# version 07
# baseline (speedup 1.0000x reference)
import os
import numpy as np
import ml_dtypes
import contextlib
import concourse.bass as bass
import concourse.mybir as mybir
from concourse.bass_utils import run_bass_kernel_spmd

F32 = mybir.dt.float32
BF16 = mybir.dt.bfloat16
AF = mybir.ActivationFunctionType
ALU = mybir.AluOpType
AXL = mybir.AxisListType

B, S, D = 4, 2048, 2048
NCORE = 8
TPC = 1024
HALF = 512
NT = TPC // 128
ND = D // 128
WARM = 128
INTER = 8192
RINTER = 1024
NSH = INTER // 128
NRES = 56
NR = RINTER // 128
NE = 8


class Prog:

    def __init__(self):
        self.ops = []
        self.marks = {}
        self.counts = {k: 0 for k in ("PE", "DVE", "ACT", "POOL", "SP")}

    def mark(self, name):
        self.marks[name] = len(self.ops)

    def op(self, eng, fn, deps=(), inc=1):
        deps = [d for d in deps if d is not None]
        self.counts[eng] += inc
        self.ops.append((eng, fn, deps, inc, eng))
        return (eng, self.counts[eng]) if inc else None

    def dma(self, queue, chan, fn, deps=()):
        deps = [d for d in deps if d is not None]
        host = {"ACTQ": "ACT", "POOLQ": "POOL", "SPQ": "SP"}[queue]
        self.counts[chan] = self.counts.get(chan, 0) + 16
        self.ops.append((host, fn, deps, 16, chan))
        return (chan, self.counts[chan])


def build_program(stage=None):
    nc = bass.Bass("TRN2", target_bir_lowering=False, debug=False)

    x_own = nc.dram_tensor("x_own", [TPC, D], F32, kind="ExternalInput").ap()
    x_warm = nc.dram_tensor("x_warm", [WARM, D], F32, kind="ExternalInput").ap()
    w_attn = nc.dram_tensor("w_attn", [D, D], F32, kind="ExternalInput").ap()
    b_attn = nc.dram_tensor("b_attn", [D], F32, kind="ExternalInput").ap()
    router_w = nc.dram_tensor("router_w", [NE, D], F32, kind="ExternalInput").ap()
    sh_gate = nc.dram_tensor("sh_gate", [INTER, D], F32, kind="ExternalInput").ap()
    sh_up = nc.dram_tensor("sh_up", [INTER, D], F32, kind="ExternalInput").ap()
    sh_down = nc.dram_tensor("sh_down", [D, INTER], F32, kind="ExternalInput").ap()
    r_gate = nc.dram_tensor("r_gate", [RINTER, D], F32, kind="ExternalInput").ap()
    r_up = nc.dram_tensor("r_up", [RINTER, D], F32, kind="ExternalInput").ap()
    r_down = nc.dram_tensor("r_down", [D, RINTER], F32, kind="ExternalInput").ap()
    out_own = nc.dram_tensor("out_own", [TPC, D], F32, kind="ExternalOutput").ap()

    h1_park = nc.dram_tensor("h1_park", [D, TPC], F32).ap()
    r2_park = nc.dram_tensor("r2_park", [D, TPC], F32).ap()
    yr_park = nc.dram_tensor("yr_park", [RINTER, TPC], BF16).ap()
    ysh_park = nc.dram_tensor("ysh_park", [(NSH - NRES) * 128, TPC], BF16).ap()
    comb_dram = nc.dram_tensor("comb_dram", [1, TPC], F32).ap()
    down_bf = nc.dram_tensor("down_bf", [D, INTER], BF16).ap()
    rdown_bf = nc.dram_tensor("rdown_bf", [D, RINTER], BF16).ap()
    r2hi_park = nc.dram_tensor("r2hi_park", [D, TPC], BF16).ap()
    r2lo_park = nc.dram_tensor("r2lo_park", [D, TPC], BF16).ap()

    ident_f_d = nc.inline_tensor(np.eye(128, dtype=np.float32), "c_identf")
    ident_b_d = nc.inline_tensor(
        np.eye(128, dtype=np.float32).astype(ml_dtypes.bfloat16), "c_identb")
    ones_cb_d = nc.inline_tensor(
        np.ones((128, 1), dtype=np.float32).astype(ml_dtypes.bfloat16), "c_onescb")
    ones_rf_d = nc.inline_tensor(np.ones((1, 128), dtype=np.float32), "c_onesrf")
    eps_d = nc.inline_tensor(np.full((128, 1), 1e-5, dtype=np.float32), "c_eps")

    MAPSIZE = 210688
    _guard = nc.alloc_sbuf_tensor("map_guard", [128, MAPSIZE], mybir.dt.uint8)
    SBUF_BASE = nc.lookup_mloc(_guard).addr
    def at(name, off, shape, dtype):
        return nc.alloc_sbuf_tensor_at(name, shape, dtype, offset=SBUF_BASE + off)

    ident_f = at("ident_f", 0, [128, 128], F32)
    ident_b = at("ident_b", 512, [128, 128], BF16)
    ones_cb = at("ones_cb", 768, [128, 1], BF16)
    bias_cols = at("bias_cols", 800, [128, 16], F32)
    sinit = at("sinit", 864, [128, 16], F32)
    ones_rf = at("ones_rf", 928, [1, 128], F32)
    routerT = at("routerT", 1440, [128, 128], BF16)
    sum_row = at("sum_row", 1696, [1, 1024], F32)
    sq_row = at("sq_row", 5792, [1, 1024], F32)
    wnat = [at(f"wnat{i}", 9888 + i * 8192, [128, 2048], BF16) for i in range(2)]
    wT = [at(f"wT{i}", 13984 + i * 8192, [128, 2048], BF16) for i in range(2)]
    rdn_nat = at("rdn_nat", 26272, [128, 1024], BF16)
    rdnT = at("rdnT", 28320, [128, 1024], BF16)
    r1b_t = at("r1b_t", 30368, [128, 1024], BF16)
    sqb_t = at("sqb_t", 32416, [128, 1024], BF16)
    scr = [at(f"scr{i}", 34464 + i * 4096, [128, 1024], F32) for i in range(6)]
    lo_scr = [at(f"lo_scr{i}", 34464 + 4 * 4096 + i * 2048, [128, 1024], BF16)
              for i in range(2)]
    gub = [at(f"gub{i}", 34464 + i * 4096, [128, 2048], BF16) for i in range(4)]
    silu01 = [at(f"silu{i}", 59040 + i * 2048, [128, 512], F32) for i in range(2)]
    r1b_t2 = at("r1b_t2", 59040, [128, 1024], BF16)
    sqb_t2 = at("sqb_t2", 61088, [128, 1024], BF16)
    RX = 63136
    x_b = at("x_b", RX, [128, ND * 1024], BF16)
    y_r2 = at("y_r2", RX, [128, NR * 1024], BF16)
    yback = [at(f"yback{i}", RX + 16384 + i * 2048, [128, 1024], BF16)
             for i in range(NSH - NRES)]
    RY = 95904
    x_fm = at("x_fm", RY, [128, ND * 1024], F32)
    xw_fm = at("xw_fm", RY + 65536, [128, ND * WARM], F32)
    xw_b = at("xw_b", RY + 73728, [128, ND * WARM], BF16)
    meanB = at("meanB", RY + 77824, [128, 1024], F32)
    rstdB = at("rstdB", RY + 81920, [128, 1024], F32)
    xtm = at("xtm", RY + 86016, [128, 2048], F32)
    xtm2 = at("xtm2", RY + 77824, [128, 2048], F32)
    wnat2 = at("wnat2", RY + 86016, [128, 2048], BF16)
    wT2 = at("wT2", RY + 90112, [128, 2048], BF16)
    y_r = at("y_r", RY, [128, NR * 1024], BF16)
    comb_bcast = at("comb_bcast", RY + 16384, [128, 1024], F32)
    logits_b = at("logits_b", RY + 20480, [8, 1024], BF16)
    lg_scr = at("lg_scr", RY + 22528, [128, 8], F32)
    eq_scr = at("eq_scr", RY + 22560, [128, 8], F32)
    m1c = at("m1c", RY + 22592, [128, 1], F32)
    m2c = at("m2c", RY + 22624, [128, 1], F32)
    cntc = at("cntc", RY + 22656, [128, 1], F32)
    c2c = at("c2c", RY + 22688, [128, 1], F32)
    v2c = at("v2c", RY + 22720, [128, 1], F32)
    e2c = at("e2c", RY + 22752, [128, 1], F32)
    rzc = at("rzc", RY + 22784, [128, 1], F32)
    comb_cols = at("comb_cols", RY + 22816, [128, 8], F32)
    comb_row = at("comb_row", RY + 22848, [1, 1024], F32)
    y_sh = at("y_sh", RY, [128, NRES * 1024], BF16)
    hiT = [at(f"hiT{i}", RY + i * 4096, [128, 2048], BF16) for i in range(2)]
    loT = [at(f"loT{i}", RY + 8192 + i * 4096, [128, 2048], BF16) for i in range(2)]
    rsum = [at(f"rsum{i}", RY + 16384 + i * 8192, [128, 2048], F32) for i in range(2)]
    out_scr = [at(f"out_scr{i}", RY + 32768 + i * 8192, [128, 2048], F32)
               for i in range(2)]
    mean_cols = at("mean_cols", RY + 49152, [128, 8], F32)
    rstd_cols = at("rstd_cols", RY + 49184, [128, 8], F32)
    bias2_cols = at("bias2_cols", RY + 49216, [128, 8], F32)
    eps_c = at("eps_c", 210624, [128, 1], F32)
    nc.const_aps.register(F32, 1e-5, eps_c.ap()[:, :])

    P = Prog()
    with contextlib.ExitStack() as stk:
        PT = stk.enter_context(nc.psum_tensor("PT", [128, 2048], F32))
        PH = [stk.enter_context(nc.psum_tensor(f"PH{i}", [128, 512], F32))
              for i in range(3)]
        PBt = stk.enter_context(nc.psum_tensor("PBt", [128, 1024], BF16))

        ph_free = [None, None, None]
        ph_ctr = [0]

        def next_ph():
            i = ph_ctr[0] % 3
            ph_ctr[0] += 1
            return i

        P.dma("ACTQ", "cCONST", lambda: nc.scalar.dma_start(out=ident_f[:, :], in_=ident_f_d.ap()[:, :]))
        P.dma("ACTQ", "cCONST", lambda: nc.scalar.dma_start(out=ident_b[:, :], in_=ident_b_d.ap()[:, :]))
        P.dma("ACTQ", "cCONST", lambda: nc.scalar.dma_start(out=ones_cb[:, :], in_=ones_cb_d.ap()[:, :]))
        P.dma("ACTQ", "cCONST", lambda: nc.scalar.dma_start(out=ones_rf[:, :], in_=ones_rf_d.ap()[:, :]))
        P.dma("ACTQ", "cCONST", lambda: nc.scalar.dma_start(out=eps_c[:, :], in_=eps_d.ap()[:, :]))
        for k in range(ND):
            dst = bias_cols[:, k:k + 1]
            src = b_attn[k * 128:(k + 1) * 128].rearrange("(p o) -> p o", o=1)
            st_consts = P.dma("ACTQ", "cCONST", lambda dst=dst, src=src: nc.scalar.dma_start(out=dst, in_=src))
        st_if = st_ib = st_oc = st_or = st_eps = st_bias = st_consts

        def tr_in(st_src, src, fm_dst_fn, bf_dst_fn, extra=(), prev_evs=None):
            trs, evs, cvs = [], [], []
            for dc in range(ND):
                po = PT[:, dc * 128:(dc + 1) * 128]
                si = src[:, dc * 128:(dc + 1) * 128]
                bank_war = [prev_evs[(dc // 4) * 4 + 3]] if prev_evs else []
                trs.append(P.op("PE", lambda po=po, si=si: nc.tensor.transpose(
                    po, si, ident_f[:, :]), deps=[st_src, st_if] + list(extra) + bank_war))
            for dc in range(ND):
                po = PT[:, dc * 128:(dc + 1) * 128]
                fd = fm_dst_fn(dc)
                evs.append(P.op("DVE", lambda fd=fd, po=po: nc.vector.tensor_copy(fd, po),
                                deps=[trs[dc | 3]]))
                bd = bf_dst_fn(dc)
                cvs.append(P.op("ACT", lambda bd=bd, fd=fd: nc.scalar.copy(bd, fd),
                                deps=[evs[dc]]))
            return trs, evs, cvs

        st_xw_l = P.dma("ACTQ", "cXTM", lambda: nc.scalar.dma_start(out=xtm[:, :], in_=x_warm[:, :]))
        xw_tr, xw_ev, xw_cv = tr_in(
            st_xw_l, xtm,
            lambda dc: xw_fm[:, dc * WARM:(dc + 1) * WARM],
            lambda dc: xw_b[:, dc * WARM:(dc + 1) * WARM])
        x_tr_last = xw_tr[-1]
        prev_evac = xw_ev
        prev_read = [xw_tr[-1], None]
        xtms = [xtm, xtm2]
        x_ev_all, x_cv_all = [], []
        for tt in range(NT):
            xt = xtms[tt % 2]
            src = x_own[tt * 128:(tt + 1) * 128, :]
            stl = P.dma("ACTQ", f"cXTM{tt % 2}", lambda src=src, xt=xt: nc.scalar.dma_start(out=xt[:, :], in_=src),
                        deps=[prev_read[tt % 2]])
            trs, evs, cvs = tr_in(
                stl, xt,
                lambda dc, tt=tt: x_fm[:, dc * 1024 + tt * 128: dc * 1024 + tt * 128 + 128],
                lambda dc, tt=tt: x_b[:, dc * 1024 + tt * 128: dc * 1024 + tt * 128 + 128],
                extra=[x_tr_last], prev_evs=prev_evac)
            prev_evac = evs
            x_tr_last = trs[-1]
            prev_read[tt % 2] = trs[-1]
            x_ev_all += evs
            x_cv_all += cvs
        P.mark("A")
        st_x_fm = x_ev_all[-1]
        st_x_b = x_cv_all[-1]
        st_xw_fm = xw_ev[-1]
        st_xw_b = xw_cv[-1]
        st_pt_clear = x_ev_all[-1]

        wring_nat = [wnat[0], wnat[1], wnat2]
        wring_T = [wT[0], wT[1], wT2]
        wnat_free = [None, None, x_tr_last]
        wT_free = [None, None, x_tr_last]

        def stream_strip(src_ap, ri, nblocks=16):
            nat, wTt = wring_nat[ri], wring_T[ri]
            cols = nblocks * 128
            stc = P.dma("POOLQ", f"cN{ri}", lambda nat=nat, src_ap=src_ap, cols=cols:
                        nc.gpsimd.dma_start(out=nat[:, 0:cols], in_=src_ap),
                        deps=[wnat_free[ri]])
            dst = wTt.ap()[:, 0:cols].rearrange("p (c i) -> p c i", c=nblocks)
            stT = P.dma("SPQ", f"cT{ri}", lambda dst=dst, nat=nat, cols=cols: nc.sync.dma_start(
                out=dst, in_=nat[:, 0:cols], transpose=True),
                deps=[stc, wT_free[ri]])
            wnat_free[ri] = stT
            return stT, wTt

        st_ppre = {}
        st_ppre_r = {}

        def emit_ppre(kd, quarters=range(4)):
            for q in quarters:
                srcw = sh_down[kd * 128:(kd + 1) * 128, q * 2048:(q + 1) * 2048]
                dstw = down_bf[kd * 128:(kd + 1) * 128, q * 2048:(q + 1) * 2048]
                st_ppre[kd] = P.dma("POOLQ", f"cPP{kd}", lambda dstw=dstw, srcw=srcw:
                                    nc.gpsimd.dma_start(out=dstw, in_=srcw))

        def emit_ppre_r(half, quarters=range(4)):
            for q in quarters:
                srcw = r_down[half * 1024 + q * 256: half * 1024 + (q + 1) * 256, :]
                dstw = rdown_bf[half * 1024 + q * 256: half * 1024 + (q + 1) * 256, :]
                st_ppre_r[half] = P.dma("POOLQ", f"cPPR{half}", lambda dstw=dstw, srcw=srcw:
                                        nc.gpsimd.dma_start(out=dstw, in_=srcw))

        lam = [scr[0], scr[1]]
        bsc = [scr[2], scr[3]]
        ysc = [scr[4], scr[5]]
        lam_free = [None, None]
        bsc_free = [None, None]
        ysc_free = [None, None]
        st_r1 = [None] * ND
        st_stats_pe = None
        pending_stats = None
        statsb = [(r1b_t, sqb_t), (r1b_t2, sqb_t2)]
        statsb_free = [None, None]

        def emit_stats_stage(k, st_src, src_buf=None):
            rb, qb = statsb[k % 2]
            sb_src = (lambda: x_fm[:, k * 1024:(k + 1) * 1024]) if src_buf is None else src_buf
            st_conv = P.op("ACT", lambda rb=rb, sb_src=sb_src: nc.scalar.copy(
                rb[:, :], sb_src()), deps=[st_src, statsb_free[k % 2]])
            st_sq = P.op("ACT", lambda qb=qb, sb_src=sb_src: nc.scalar.activation(
                out=qb[:, :], in_=sb_src(), func=AF.Square, scale=1.0),
                deps=[st_src, statsb_free[k % 2]])
            last = None
            for h in range(2):
                po = PT[0:1, h * 512:(h + 1) * 512]
                rh = rb[:, h * HALF:(h + 1) * HALF]
                last = P.op("PE", lambda po=po, rh=rh, k=k: nc.tensor.matmul(
                    po, ones_cb[:, :], rh, start=(k == 0), stop=(k == ND - 1)),
                    deps=[st_conv, st_oc] + ([st_pt_clear] if k == 0 else []), inc=0)
                po2 = PT[0:1, 1024 + h * 512: 1536 + h * 512]
                rh2 = qb[:, h * HALF:(h + 1) * HALF]
                last = P.op("PE", lambda po2=po2, rh2=rh2, k=k: nc.tensor.matmul(
                    po2, ones_cb[:, :], rh2, start=(k == 0), stop=(k == ND - 1)),
                    deps=[st_sq], inc=(1 if h == 1 else 0))
            statsb_free[k % 2] = last
            return last

        for k in range(ND):
            r = k % 3
            p2 = k % 2
            stT, wTt = stream_strip(w_attn[k * 128:(k + 1) * 128, :], r)
            lam_t, b_t, ys_t = lam[p2], bsc[p2], ysc[p2]

            zw = next_ph()
            mm = None
            for dc in range(ND):
                lhs = wTt[:, dc * 128:(dc + 1) * 128]
                rhs = xw_b[:, dc * WARM:(dc + 1) * WARM]
                mm = P.op("PE", lambda lhs=lhs, rhs=rhs, zw=zw, dc=dc: nc.tensor.matmul(
                    PH[zw][:, 0:WARM], lhs, rhs, start=(dc == 0), stop=(dc == ND - 1)),
                    deps=([stT, st_xw_b, ph_free[zw]] if dc == 0 else []),
                    inc=(1 if dc == ND - 1 else 0))
            st_sigw = P.op("ACT", lambda zw=zw, k=k, lam_t=lam_t: nc.scalar.activation(
                out=lam_t[:, 0:WARM], in_=PH[zw][:, 0:WARM], func=AF.Sigmoid,
                bias=bias_cols[:, k:k + 1], scale=1.0),
                deps=[mm, st_bias, lam_free[p2]])
            ph_free[zw] = st_sigw
            st_b1 = P.op("DVE", lambda lam_t=lam_t, b_t=b_t: nc.vector.tensor_scalar(
                out=b_t[:, 0:WARM], in0=lam_t[:, 0:WARM], scalar1=-1.0, scalar2=1.0,
                op0=ALU.mult, op1=ALU.add), deps=[st_sigw, bsc_free[p2]])
            st_b2 = P.op("DVE", lambda b_t=b_t, k=k: nc.vector.tensor_mul(
                b_t[:, 0:WARM], b_t[:, 0:WARM], xw_fm[:, k * WARM:(k + 1) * WARM]),
                deps=[st_b1, st_xw_fm])
            st_scw = P.op("DVE", lambda lam_t=lam_t, b_t=b_t, ys_t=ys_t:
                          nc.vector.tensor_tensor_scan(
                out=ys_t[:, 0:WARM], data0=lam_t[:, 0:WARM], data1=b_t[:, 0:WARM],
                initial=0.0, op0=ALU.mult, op1=ALU.add),
                deps=[st_b2, ysc_free[p2]])
            st_si = P.op("DVE", lambda ys_t=ys_t, k=k: nc.vector.tensor_copy(
                sinit[:, k:k + 1], ys_t[:, WARM - 1:WARM]), deps=[st_scw])

            st_sig = [None, None]
            for h in range(2):
                zi = next_ph()
                mm = None
                for dc in range(ND):
                    lhs = wTt[:, dc * 128:(dc + 1) * 128]
                    rhs = x_b[:, dc * 1024 + h * HALF: dc * 1024 + (h + 1) * HALF]
                    mm = P.op("PE", lambda lhs=lhs, rhs=rhs, zi=zi, dc=dc: nc.tensor.matmul(
                        PH[zi][:, :], lhs, rhs, start=(dc == 0), stop=(dc == ND - 1)),
                        deps=([stT, st_x_b, ph_free[zi]] if dc == 0 else []),
                        inc=(1 if dc == ND - 1 else 0))
                if h == 1:
                    wT_free[r] = mm
                st_sig[h] = P.op("ACT", lambda zi=zi, h=h, k=k, lam_t=lam_t:
                                 nc.scalar.activation(
                    out=lam_t[:, h * HALF:(h + 1) * HALF], in_=PH[zi][:, :],
                    func=AF.Sigmoid, bias=bias_cols[:, k:k + 1], scale=1.0),
                    deps=[mm, st_scw])
                ph_free[zi] = st_sig[h]
            st_b1 = P.op("DVE", lambda lam_t=lam_t, b_t=b_t: nc.vector.tensor_scalar(
                out=b_t[:, :], in0=lam_t[:, :], scalar1=-1.0, scalar2=1.0,
                op0=ALU.mult, op1=ALU.add), deps=[st_sig[0], st_sig[1], st_scw])
            st_b2 = P.op("DVE", lambda b_t=b_t, k=k: nc.vector.tensor_mul(
                b_t[:, :], b_t[:, :], x_fm[:, k * 1024:(k + 1) * 1024]),
                deps=[st_b1, st_x_fm])
            st_scan = P.op("DVE", lambda lam_t=lam_t, b_t=b_t, ys_t=ys_t, k=k:
                           nc.vector.tensor_tensor_scan(
                out=ys_t[:, :], data0=lam_t[:, :], data1=b_t[:, :],
                initial=sinit[:, k:k + 1], op0=ALU.mult, op1=ALU.add),
                deps=[st_b2, st_si])
            lam_free[p2] = st_scan
            bsc_free[p2] = st_scan
            st_r1[k] = P.op("DVE", lambda ys_t=ys_t, k=k: nc.vector.tensor_add(
                x_fm[:, k * 1024:(k + 1) * 1024],
                x_fm[:, k * 1024:(k + 1) * 1024], ys_t[:, :]), deps=[st_scan])
            ysc_free[p2] = st_r1[k]

            if pending_stats is not None:
                st_stats_pe = emit_stats_stage(*pending_stats)
            pending_stats = (k, st_r1[k])
            if k in (6, 8, 10, 12, 14):
                emit_ppre((k - 6) // 2)

        if pending_stats is not None:
            st_stats_pe = emit_stats_stage(*pending_stats)
            pending_stats = None
        st_rw = P.dma("POOLQ", "cN0", lambda: nc.gpsimd.dma_start(
            out=wnat[0][0:8, :], in_=router_w[:, :]), deps=[wnat_free[0]])
        wnat_free[0] = st_rw
        P.mark("C")
        st_sumr = P.op("DVE", lambda: nc.vector.tensor_copy(
            sum_row[:, :], PT[0:1, 0:1024]), deps=[st_stats_pe])
        st_sqr = P.op("DVE", lambda: nc.vector.tensor_copy(
            sq_row[:, :], PT[0:1, 1024:2048]), deps=[st_stats_pe])
        st_mean = P.op("DVE", lambda: nc.vector.tensor_scalar_mul(
            sum_row[:, :], sum_row[:, :], 1.0 / D), deps=[st_sumr])
        st_msq = P.op("DVE", lambda: nc.vector.tensor_scalar_mul(
            sq_row[:, :], sq_row[:, :], 1.0 / D), deps=[st_sqr])
        st_m2 = P.op("DVE", lambda: nc.vector.tensor_tensor(
            scr[0][0:1, :], sum_row[:, :], sum_row[:, :], op=ALU.mult),
            deps=[st_mean, lam_free[0]])
        st_var = P.op("DVE", lambda: nc.vector.tensor_sub(
            sq_row[:, :], sq_row[:, :], scr[0][0:1, :]), deps=[st_m2, st_msq])
        st_std = P.op("ACT", lambda: nc.scalar.activation(
            out=sq_row[:, :], in_=sq_row[:, :], func=AF.Sqrt, bias=1e-5, scale=1.0),
            deps=[st_var, st_eps])
        st_rstd = P.op("DVE", lambda: nc.vector.reciprocal(
            sq_row[:, :], sq_row[:, :]), deps=[st_std])
        st_bc = {}
        for nm, row_ap, dep, dstb in (("m", sum_row, st_mean, meanB),
                                      ("r", sq_row, st_rstd, rstdB)):
            for h in range(2):
                pi = next_ph()
                srow = row_ap[0:1, h * HALF:(h + 1) * HALF]
                mm = P.op("PE", lambda pi=pi, srow=srow: nc.tensor.matmul(
                    PH[pi][:, :], ones_rf[:, :], srow, start=True, stop=True),
                    deps=[dep, st_or, ph_free[pi]])
                dst = dstb[:, h * HALF:(h + 1) * HALF]
                ev = P.op("DVE", lambda dst=dst, pi=pi: nc.vector.tensor_copy(
                    dst, PH[pi][:, :]), deps=[mm])
                ph_free[pi] = ev
                st_bc[(nm, h)] = ev
        st_meanB = st_bc[("m", 1)]
        st_rstdB = st_bc[("r", 1)]
        nrm2 = [scr[1], scr[5]]
        st_nrm_free = [lam_free[1], ysc_free[1]]
        st_h1 = [None] * ND
        st_h1b = [None] * ND
        st_h1pk = [None] * ND
        for k in range(ND):
            p2 = k % 2
            nrm = nrm2[p2]
            st_s1 = P.op("DVE", lambda k=k, nrm=nrm: nc.vector.tensor_sub(
                nrm[:, :], x_fm[:, k * 1024:(k + 1) * 1024], meanB[:, :]),
                deps=[st_r1[k], st_meanB, st_bc[("m", 0)], st_nrm_free[p2]])
            eng = "DVE" if p2 == 0 else "POOL"
            fn = nc.vector.tensor_mul if p2 == 0 else nc.gpsimd.tensor_mul
            st_h1[k] = P.op(eng, lambda k=k, nrm=nrm, fn=fn: fn(
                x_fm[:, k * 1024:(k + 1) * 1024], nrm[:, :], rstdB[:, :]),
                deps=[st_s1, st_rstdB, st_bc[("r", 0)]])
            st_nrm_free[p2] = st_h1[k]
            st_h1b[k] = P.op("ACT", lambda k=k: nc.scalar.copy(
                x_b[:, k * 1024:(k + 1) * 1024], x_fm[:, k * 1024:(k + 1) * 1024]),
                deps=[st_h1[k], st_x_b])
            dst = h1_park[k * 128:(k + 1) * 128, :]
            st_h1pk[k] = P.dma("ACTQ", "cH1P", lambda k=k, dst=dst: nc.scalar.dma_start(
                out=dst, in_=x_fm[:, k * 1024:(k + 1) * 1024]), deps=[st_h1[k]])
            if k in (5, 10, 15):
                emit_ppre(5 + k // 5 - 1)
        st_h1b_all = st_h1b[-1]
        st_h1_parked = st_h1pk[-1]

        P.mark("D")
        st_rTt = []
        for dc in range(ND):
            po = PBt[:, dc * 8:(dc + 1) * 8]
            si = wnat[0][0:8, dc * 128:(dc + 1) * 128]
            st_rTt.append(P.op("PE", lambda po=po, si=si: nc.tensor.transpose(
                po, si, ident_b[0:8, 0:8]), deps=[st_rw, st_ib]))
        st_rT = []
        for dc in range(ND):
            po = PBt[:, dc * 8:(dc + 1) * 8]
            st_rT.append(P.op("ACT", lambda dc=dc, po=po: nc.scalar.copy(
                routerT[:, dc * 8:(dc + 1) * 8], po), deps=[st_rTt[-1]]))
        wnat_free[0] = st_rT[-1]
        pb_free = st_rT[-1]
        st_rT_all = st_rT[-1]
        st_lg = [None, None]
        for h in range(2):
            pi = next_ph()
            mm = None
            for dc in range(ND):
                lhs = routerT[:, dc * 8:(dc + 1) * 8]
                rhs = x_b[:, dc * 1024 + h * HALF: dc * 1024 + (h + 1) * HALF]
                mm = P.op("PE", lambda lhs=lhs, rhs=rhs, pi=pi, dc=dc: nc.tensor.matmul(
                    PH[pi][0:8, :], lhs, rhs, start=(dc == 0), stop=(dc == ND - 1)),
                    deps=([st_rT_all, st_h1b_all, ph_free[pi]] if dc == 0 else []),
                    inc=(1 if dc == ND - 1 else 0))
            st_lg[h] = P.op("ACT", lambda h=h, pi=pi: nc.scalar.copy(
                logits_b[0:8, h * HALF:(h + 1) * HALF], PH[pi][0:8, :]),
                deps=[mm, st_h1_parked])
            ph_free[pi] = st_lg[h]
        silu_free = [st_stats_pe, st_stats_pe]

        def gu_strips(wg, wu, nstrips, ydst_fn, scale, scale_dep, ydep_fn,
                      park_fn=None, order=None, hook=None, rings=None):
            (gnat, gnatF, gT, gTF, unat, unatF, uT, uTF, tag) = rings
            depth = len(gnat)

            def flat(x):
                if x is None:
                    return []
                if isinstance(x, tuple) and len(x) == 2 and isinstance(x[0], str):
                    return [x]
                return [d for d in x if d is not None]

            st_y_all = []
            for si, s in enumerate(order if order is not None else range(nstrips)):
                if hook is not None and si in hook:
                    hook[si]()
                ydeps = ydep_fn(s)
                ri = si % depth
                natg, wTg = gnat[ri], gT[ri]
                sg = wg[s * 128:(s + 1) * 128, :]
                stcg = P.dma("POOLQ", f"c{tag}gN{ri}", lambda natg=natg, sg=sg:
                             nc.gpsimd.dma_start(out=natg[:, :], in_=sg),
                             deps=flat(gnatF[ri]))
                dstg = wTg.ap().rearrange("p (c i) -> p c i", c=16)
                stTg = P.dma("SPQ", f"c{tag}gT{ri}", lambda dstg=dstg, natg=natg:
                             nc.sync.dma_start(out=dstg, in_=natg[:, :], transpose=True),
                             deps=[stcg] + flat(gTF[ri]))
                gnatF[ri] = stTg
                natu, wTu = unat[ri], uT[ri]
                su = wu[s * 128:(s + 1) * 128, :]
                stcu = P.dma("POOLQ", f"c{tag}uN{ri}", lambda natu=natu, su=su:
                             nc.gpsimd.dma_start(out=natu[:, :], in_=su),
                             deps=flat(unatF[ri]))
                dstu = wTu.ap().rearrange("p (c i) -> p c i", c=16)
                stTu = P.dma("SPQ", f"c{tag}uT{ri}", lambda dstu=dstu, natu=natu:
                             nc.sync.dma_start(out=dstu, in_=natu[:, :], transpose=True),
                             deps=[stcu] + flat(uTF[ri]))
                unatF[ri] = stTu
                for h in range(2):
                    gi = next_ph()
                    mmg = None
                    for dc in range(ND):
                        lhs = wTg[:, dc * 128:(dc + 1) * 128]
                        rhs = x_b[:, dc * 1024 + h * HALF: dc * 1024 + (h + 1) * HALF]
                        mmg = P.op("PE", lambda lhs=lhs, rhs=rhs, gi=gi, dc=dc:
                                   nc.tensor.matmul(
                            PH[gi][:, :], lhs, rhs, start=(dc == 0), stop=(dc == ND - 1)),
                            deps=([stTg, st_h1b_all, ph_free[gi]] if dc == 0 else []),
                            inc=(1 if dc == ND - 1 else 0))
                    ui = next_ph()
                    mmu = None
                    for dc in range(ND):
                        lhs = wTu[:, dc * 128:(dc + 1) * 128]
                        rhs = x_b[:, dc * 1024 + h * HALF: dc * 1024 + (h + 1) * HALF]
                        mmu = P.op("PE", lambda lhs=lhs, rhs=rhs, ui=ui, dc=dc:
                                   nc.tensor.matmul(
                            PH[ui][:, :], lhs, rhs, start=(dc == 0), stop=(dc == ND - 1)),
                            deps=([stTu, ph_free[ui]] if dc == 0 else []),
                            inc=(1 if dc == ND - 1 else 0))
                    if h == 1:
                        gTF[ri] = mmg
                        uTF[ri] = mmu
                    sscr = silu01[h]
                    st_s = P.op("ACT", lambda gi=gi, sscr=sscr: nc.scalar.activation(
                        out=sscr[:, :], in_=PH[gi][:, :], func=AF.Silu, scale=1.0),
                        deps=[mmg, silu_free[h]])
                    ph_free[gi] = st_s
                    ydst = ydst_fn(s, h)
                    if scale is None:
                        st_y = P.op("DVE", lambda ydst=ydst, sscr=sscr, ui=ui:
                                    nc.vector.tensor_tensor(
                            ydst, sscr[:, :], PH[ui][:, :], op=ALU.mult),
                            deps=[st_s, mmu] + list(ydeps))
                    else:
                        st_y0 = P.op("DVE", lambda sscr=sscr, ui=ui:
                                     nc.vector.tensor_tensor(
                            sscr[:, :], sscr[:, :], PH[ui][:, :], op=ALU.mult),
                            deps=[st_s, mmu])
                        sc_ap = scale[:, h * HALF:(h + 1) * HALF]
                        st_y = P.op("DVE", lambda ydst=ydst, sscr=sscr, sc_ap=sc_ap:
                                    nc.vector.tensor_mul(ydst, sscr[:, :], sc_ap),
                                    deps=[st_y0, scale_dep] + list(ydeps))
                    ph_free[ui] = st_y
                    silu_free[h] = st_y
                    st_y_all.append(st_y)
                if park_fn is not None:
                    park_fn(s, st_y_all[-2], st_y_all[-1])
            return st_y_all

        def e0_ppre(s, a, b):
            if s in (2, 4, 6):
                emit_ppre(8 + s // 2 - 1)

        gu_rings = (
            [wnat[0], gub[0]], [wnat_free[0], (lam_free[0], st_var)],
            [wT[0], gub[1]], [wT_free[0], st_nrm_free[0]],
            [wnat[1], gub[2]], [wnat_free[1], bsc_free[0]],
            [wT[1], gub[3]], [wT_free[1], bsc_free[1]], "G")
        st_yr_unscaled = gu_strips(
            r_gate, r_up, NR,
            lambda s, h: y_r[:, s * 1024 + h * HALF: s * 1024 + (h + 1) * HALF],
            None, None, lambda s: [st_h1_parked], park_fn=e0_ppre,
            rings=gu_rings)

        st_lgT = []
        for j in range(NT):
            po = PBt[:, j * 8:(j + 1) * 8]
            si = logits_b[0:8, j * 128:(j + 1) * 128]
            st_lgT.append(P.op("PE", lambda po=po, si=si: nc.tensor.transpose(
                po, si, ident_b[0:8, 0:8]), deps=[st_lg[0], st_lg[1], pb_free]))
        st_cc = []
        cr_box = [None]

        def emit_comb_j(j):
                po = PBt[:, j * 8:(j + 1) * 8]
                s0 = P.op("DVE", lambda po=po: nc.vector.tensor_copy(lg_scr[:, :], po),
                          deps=[st_lgT[-1], (st_cc[-1] if st_cc else st_h1_parked)])
                s1 = P.op("DVE", lambda: nc.vector.reduce_max(
                    m1c[:, :], lg_scr[:, :], axis=AXL.X), deps=[s0])
                s2 = P.op("DVE", lambda: nc.vector.tensor_scalar(
                    out=eq_scr[:, :], in0=lg_scr[:, :], scalar1=m1c[:, 0:1],
                    scalar2=None, op0=ALU.is_equal), deps=[s1])
                s3 = P.op("DVE", lambda: nc.vector.tensor_reduce(
                    cntc[:, :], eq_scr[:, :], axis=AXL.X, op=ALU.add), deps=[s2])
                s4 = P.op("DVE", lambda: nc.vector.scalar_tensor_tensor(
                    out=eq_scr[:, :], in0=eq_scr[:, :], scalar=-1e30, in1=lg_scr[:, :],
                    op0=ALU.mult, op1=ALU.add), deps=[s3])
                s5 = P.op("DVE", lambda: nc.vector.reduce_max(
                    m2c[:, :], eq_scr[:, :], axis=AXL.X), deps=[s4])
                s6 = P.op("DVE", lambda: nc.vector.tensor_scalar(
                    out=c2c[:, :], in0=cntc[:, :], scalar1=2.0, scalar2=None,
                    op0=ALU.is_ge), deps=[s3])
                s7 = P.op("DVE", lambda: nc.vector.tensor_sub(
                    v2c[:, :], m1c[:, :], m2c[:, :]), deps=[s5])
                s8 = P.op("DVE", lambda: nc.vector.tensor_mul(
                    v2c[:, :], v2c[:, :], c2c[:, :]), deps=[s7, s6])
                s9 = P.op("DVE", lambda: nc.vector.tensor_add(
                    v2c[:, :], v2c[:, :], m2c[:, :]), deps=[s8])
                s10 = P.op("DVE", lambda: nc.vector.tensor_scalar_mul(
                    m1c[:, :], m1c[:, :], -1.0), deps=[s9, s2])
                s11 = P.op("ACT", lambda: nc.scalar.activation(
                    out=e2c[:, :], in_=v2c[:, :], func=AF.Exp, bias=m1c[:, 0:1],
                    scale=1.0), deps=[s10])
                s12 = P.op("DVE", lambda: nc.vector.tensor_scalar_add(
                    rzc[:, :], e2c[:, :], 1.0), deps=[s11])
                s13 = P.op("DVE", lambda: nc.vector.reciprocal(
                    rzc[:, :], rzc[:, :]), deps=[s12])
                s14 = P.op("DVE", lambda j=j: nc.vector.scalar_tensor_tensor(
                    out=comb_cols[:, j:j + 1], in0=e2c[:, :], scalar=rzc[:, 0:1],
                    in1=rzc[:, :], op0=ALU.mult, op1=ALU.add), deps=[s13])
                st_cc.append(s14)

        def emit_comb_tail():
            st_cd = []
            for j in range(NT):
                dst = comb_dram[0:1, j * 128:(j + 1) * 128]
                st_cd.append(P.dma("ACTQ", "cCMB", lambda j=j, dst=dst: nc.scalar.dma_start(
                    out=dst, in_=comb_cols[:, j:j + 1]), deps=[st_cc[j]]))
            cr_box[0] = P.dma("ACTQ", "cCMB", lambda: nc.scalar.dma_start(
                out=comb_row[:, :], in_=comb_dram[:, :]), deps=[st_cd[-1]])
        yr_parked_box = [None]

        def finish_comb():
            st_cr = cr_box[0]
            st_comb = None
            for h in range(2):
                pi = next_ph()
                srow = comb_row[0:1, h * HALF:(h + 1) * HALF]
                mm = P.op("PE", lambda pi=pi, srow=srow: nc.tensor.matmul(
                    PH[pi][:, :], ones_rf[:, :], srow, start=True, stop=True),
                    deps=[st_cr, ph_free[pi]])
                dst = comb_bcast[:, h * HALF:(h + 1) * HALF]
                st_comb = P.op("DVE", lambda dst=dst, pi=pi: nc.vector.tensor_copy(
                    dst, PH[pi][:, :]), deps=[mm])
                ph_free[pi] = st_comb
            st_yr_park = [None] * NR
            for s in range(NR):
                for h in range(2):
                    ya = y_r[:, s * 1024 + h * HALF: s * 1024 + (h + 1) * HALF]
                    ca = comb_bcast[:, h * HALF:(h + 1) * HALF]
                    stsc = P.op("DVE", lambda ya=ya, ca=ca: nc.vector.tensor_mul(
                        ya, ya, ca), deps=[st_yr_unscaled[2 * s + h], st_comb])
                    if h == 1:
                        stsc1 = stsc
                    else:
                        stsc0 = stsc
                dst = yr_park[s * 128:(s + 1) * 128, :]
                srcp = y_r[:, s * 1024:(s + 1) * 1024]
                st_yr_park[s] = P.dma("ACTQ", "cYRP", lambda dst=dst, srcp=srcp:
                                      nc.scalar.dma_start(out=dst, in_=srcp),
                                      deps=[stsc0, stsc1])
            yr_parked_box[0] = st_yr_park[-1]

        P.mark("E0")
        park_stage = [rdn_nat, rdnT]
        park_stage_free = [None, None]
        st_ysh_park = {}

        def ysh_dst(s, h):
            if s < NRES:
                return y_sh[:, s * 1024 + h * HALF: s * 1024 + (h + 1) * HALF]
            return park_stage[s % 2][:, h * HALF:(h + 1) * HALF]

        ppre_pieces = ([(kd, q) for kd in range(11, ND) for q in range(4)] +
                       [("r", hq) for hq in range(8)])

        def park_ysh(s, sty0, sty1):
            if 16 <= s < 16 + len(ppre_pieces):
                kd, q = ppre_pieces[s - 16]
                if kd == "r":
                    emit_ppre_r(q // 4, quarters=[q % 4])
                else:
                    emit_ppre(kd, quarters=[q])
            if s < NRES:
                return
            t = park_stage[s % 2]
            dst = ysh_park[(s - NRES) * 128:(s - NRES + 1) * 128, :]
            st = P.dma("ACTQ", f"cYSP{s % 2}", lambda dst=dst, t=t: nc.scalar.dma_start(
                out=dst, in_=t[:, :]), deps=[sty0, sty1])
            st_ysh_park[s] = st
            park_stage_free[s % 2] = st

        e1_order = list(range(14, NSH)) + list(range(14))
        st_ysh = gu_strips(
            sh_gate, sh_up, NSH, ysh_dst, None, None,
            lambda s: [st_h1_parked] +
                      ([yr_parked_box[0]] if s < 14 else []) +
                      ([park_stage_free[s % 2]] if s >= NRES else []),
            park_fn=park_ysh, order=e1_order,
            hook={**{1 + j: (lambda j=j: emit_comb_j(j)) for j in range(NT)},
                  10: emit_comb_tail, 12: finish_comb},
            rings=gu_rings)
        st_yr_parked = yr_parked_box[0]
        st_ysh_all = st_ysh[-1]
        st_ysh_parked_all = [st_ysh_park[NSH - 2], st_ysh_park[NSH - 1]]
        st_xb_done = st_ysh_all

        P.mark("E1")
        st_yrb = []
        for s in range(NR):
            dst = y_r2[:, s * 1024:(s + 1) * 1024]
            src = yr_park[s * 128:(s + 1) * 128, :]
            st_yrb.append(P.dma("ACTQ", "cYR2", lambda dst=dst, src=src: nc.scalar.dma_start(
                out=dst, in_=src), deps=[st_yr_parked, st_xb_done]))
        st_yrb_all = st_yrb[-1]

        h1j = [bsc[0], bsc[1]]
        r2s = [lam[0], lam[1]]
        (g_gnat, g_gnatF, g_gT, g_gTF, g_unat, g_unatF, g_uT, g_uTF, _tag) = gu_rings
        h1j_free = [g_unatF[1], g_uTF[1]]
        r2s_free = [(st_var, g_gnatF[1]), (st_nrm_free[0], g_gTF[1])]
        yb_free = [st_xb_done] * (NSH - NRES)
        st_stats2_pe = None
        st_r2pk = [None] * ND
        rdn_free = [park_stage_free[0], park_stage_free[1]]
        statsb_free = [(statsb_free[0],), (silu_free[1],)]
        lo_free = [ysc_free[0], ysc_free[0]]
        pending_stats2 = None

        def emit_stats2_stage(k, p2, st_a, st_b):
            rb, qb = statsb[k % 2]
            st_conv = P.op("ACT", lambda rb=rb, p2=p2: nc.scalar.copy(
                rb[:, :], r2s[p2][:, :]),
                deps=[st_a, st_b] + list(statsb_free[k % 2]))
            st_sq = P.op("ACT", lambda qb=qb, p2=p2: nc.scalar.activation(
                out=qb[:, :], in_=r2s[p2][:, :], func=AF.Square, scale=1.0),
                deps=[st_a, st_b] + list(statsb_free[k % 2]))
            st_lo = P.op("DVE", lambda k=k, p2=p2, rb=rb: nc.vector.tensor_sub(
                lo_scr[k % 2][:, :], r2s[p2][:, :], rb[:, :]),
                deps=[st_a, st_b, st_conv, lo_free[k % 2], ysc_free[k % 2]])
            last = None
            for h in range(2):
                po = PT[0:1, h * 512:(h + 1) * 512]
                rh = rb[:, h * HALF:(h + 1) * HALF]
                last = P.op("PE", lambda po=po, rh=rh, k=k: nc.tensor.matmul(
                    po, ones_cb[:, :], rh, start=(k == 0), stop=(k == ND - 1)),
                    deps=[st_conv] + ([st_sumr, st_sqr] if k == 0 else []), inc=0)
                po2 = PT[0:1, 1024 + h * 512: 1536 + h * 512]
                rh2 = qb[:, h * HALF:(h + 1) * HALF]
                last = P.op("PE", lambda po2=po2, rh2=rh2, k=k: nc.tensor.matmul(
                    po2, ones_cb[:, :], rh2, start=(k == 0), stop=(k == ND - 1)),
                    deps=[st_sq], inc=(1 if h == 1 else 0))
            dsth = r2hi_park[k * 128:(k + 1) * 128, :]
            st_pkh = P.dma("ACTQ", f"cRH{k % 2}", lambda dsth=dsth, rb=rb:
                           nc.scalar.dma_start(out=dsth, in_=rb[:, :]),
                           deps=[st_conv])
            dstl = r2lo_park[k * 128:(k + 1) * 128, :]
            st_pkl = P.dma("ACTQ", f"cRL{k % 2}", lambda dstl=dstl, k=k:
                           nc.scalar.dma_start(out=dstl, in_=lo_scr[k % 2][:, :]),
                           deps=[st_lo])
            statsb_free[k % 2] = (last, st_pkh)
            lo_free[k % 2] = st_pkl
            st_r2pk[k] = (st_pkh, st_pkl)
            r2s_free[p2] = (st_sq, st_lo)
            return last

        dn_ring = [wT[0], wT[1], wnat[0], wnat[1]]
        dn_free = [g_gTF[0], g_uTF[0], g_gnatF[0], g_unatF[0]]
        rdn_ring = [rdn_nat, rdnT]
        rdn_dst_free = [rdn_free[0], rdn_free[1]]
        for k in range(ND):
            p2 = k % 2
            src = h1_park[k * 128:(k + 1) * 128, :]
            st_jit = P.dma("ACTQ", f"cJ{p2}", lambda src=src, p2=p2: nc.scalar.dma_start(
                out=h1j[p2][:, :], in_=src), deps=[st_h1_parked, h1j_free[p2]])
            rr = k % 2
            rdnT_t = rdn_ring[rr]
            rsrc = rdown_bf[k * 128:(k + 1) * 128, :]
            rdst = rdnT_t.ap().rearrange("p (c i) -> p c i", c=8)
            st_rT = P.dma("SPQ", f"cRT{rr}", lambda rdst=rdst, rsrc=rsrc: nc.sync.dma_start(
                out=rdst, in_=rsrc, transpose=True),
                deps=[st_ppre_r[0], st_ppre_r[1], rdn_dst_free[rr]])
            st_yb = {}
            for i, s in enumerate(range(NRES, NSH)):
                dst = yback[i]
                src2 = ysh_park[i * 128:(i + 1) * 128, :]
                st_yb[s] = P.dma("ACTQ", f"cYB{i}", lambda dst=dst, src2=src2: nc.scalar.dma_start(
                    out=dst[:, :], in_=src2),
                    deps=st_ysh_parked_all + [yb_free[i]])
            hslot = [next_ph(), next_ph()]
            mm_last = [None, None]
            st_q = None
            for q in range(4):
                ri4 = (4 * k + q) % 4
                wTq = dn_ring[ri4]
                src3 = down_bf[k * 128:(k + 1) * 128, q * 2048:(q + 1) * 2048]
                tdst = wTq.ap().rearrange("p (c i) -> p c i", c=16)
                stT = P.dma("SPQ", f"cDT{ri4}", lambda tdst=tdst, src3=src3:
                            nc.sync.dma_start(out=tdst, in_=src3, transpose=True),
                            deps=[st_ppre[k], dn_free[ri4]])
                for h in range(2):
                    pi = hslot[h]
                    for c in range(16):
                        s = q * 16 + c
                        lhs = wTq[:, c * 128:(c + 1) * 128]
                        if s < NRES:
                            rhs = y_sh[:, s * 1024 + h * HALF: s * 1024 + (h + 1) * HALF]
                            rdep = st_ysh_all if (c == 0 and h == 0) else None
                        else:
                            rhs = yback[s - NRES][:, h * HALF:(h + 1) * HALF]
                            rdep = st_yb[s]
                        is_last = (h == 1 and c == 15)
                        mm_last[h] = P.op("PE", lambda lhs=lhs, rhs=rhs, pi=pi, q=q, c=c:
                                          nc.tensor.matmul(
                            PH[pi][:, :], lhs, rhs,
                            start=(q == 0 and c == 0), stop=False),
                            deps=([stT, rdep, ph_free[pi]] if c == 0 else [rdep]),
                            inc=(1 if is_last else 0))
                        if not is_last and mm_last[h] is None:
                            mm_last[h] = None
                st_q = mm_last[1]
                dn_free[ri4] = st_q
            for i in range(NSH - NRES):
                yb_free[i] = st_q
            for h in range(2):
                pi = hslot[h]
                for c in range(NR):
                    lhs = rdnT_t[:, c * 128:(c + 1) * 128]
                    rhs = y_r2[:, c * 1024 + h * HALF: c * 1024 + (h + 1) * HALF]
                    mm_last[h] = P.op("PE", lambda lhs=lhs, rhs=rhs, pi=pi, c=c:
                                      nc.tensor.matmul(
                        PH[pi][:, :], lhs, rhs, start=False, stop=(c == NR - 1)),
                        deps=([st_rT, st_yrb_all] if c == 0 else []),
                        inc=(1 if c == NR - 1 else 0))
            rdn_dst_free[rr] = mm_last[1]
            st_r2h = [None, None]
            for h in range(2):
                pi = hslot[h]
                dst = r2s[p2][:, h * HALF:(h + 1) * HALF]
                st_r2h[h] = P.op("DVE", lambda dst=dst, pi=pi, p2=p2, h=h:
                                 nc.vector.tensor_add(
                    dst, PH[pi][:, :], h1j[p2][:, h * HALF:(h + 1) * HALF]),
                    deps=[mm_last[h], st_jit] +
                          (list(r2s_free[p2]) if h == 0 else []))
                ph_free[pi] = st_r2h[h]
            h1j_free[p2] = st_r2h[1]
            if pending_stats2 is not None:
                st_stats2_pe = emit_stats2_stage(*pending_stats2)
            pending_stats2 = (k, p2, st_r2h[0], st_r2h[1])

        if pending_stats2 is not None:
            st_stats2_pe = emit_stats2_stage(*pending_stats2)
            pending_stats2 = None
        P.mark("F")
        st_sumr2 = P.op("DVE", lambda: nc.vector.tensor_copy(
            sum_row[:, :], PT[0:1, 0:1024]), deps=[st_stats2_pe, st_meanB])
        st_sqr2 = P.op("DVE", lambda: nc.vector.tensor_copy(
            sq_row[:, :], PT[0:1, 1024:2048]), deps=[st_stats2_pe, st_rstdB])
        st_mean2 = P.op("DVE", lambda: nc.vector.tensor_scalar_mul(
            sum_row[:, :], sum_row[:, :], 1.0 / D), deps=[st_sumr2])
        st_msq2 = P.op("DVE", lambda: nc.vector.tensor_scalar_mul(
            sq_row[:, :], sq_row[:, :], 1.0 / D), deps=[st_sqr2])
        st_m22 = P.op("DVE", lambda: nc.vector.tensor_tensor(
            scr[4][0:1, :], sum_row[:, :], sum_row[:, :], op=ALU.mult),
            deps=[st_mean2, ysc_free[0], st_r2pk[-1][1], st_r2pk[-2][1]])
        st_var2 = P.op("DVE", lambda: nc.vector.tensor_sub(
            sq_row[:, :], sq_row[:, :], scr[4][0:1, :]), deps=[st_m22, st_msq2])
        st_std2 = P.op("ACT", lambda: nc.scalar.activation(
            out=sq_row[:, :], in_=sq_row[:, :], func=AF.Sqrt, bias=1e-5, scale=1.0),
            deps=[st_var2, st_eps])
        st_rstd2 = P.op("DVE", lambda: nc.vector.reciprocal(
            sq_row[:, :], sq_row[:, :]), deps=[st_std2])
        st_mc = []
        for j in range(NT):
            po = PH[0][:, j:j + 1]
            si = sum_row[0:1, j * 128:(j + 1) * 128]
            stm = P.op("PE", lambda po=po, si=si: nc.tensor.transpose(
                po, si, ident_f[0:1, 0:1]), deps=[st_mean2, ph_free[0]])
            po2 = PH[1][:, j:j + 1]
            si2 = sq_row[0:1, j * 128:(j + 1) * 128]
            stm2 = P.op("PE", lambda po2=po2, si2=si2: nc.tensor.transpose(
                po2, si2, ident_f[0:1, 0:1]), deps=[st_rstd2, ph_free[1]])
            st_mc.append((stm, stm2))
        st_pk_all = list(st_r2pk[-2]) + list(st_r2pk[-1])
        st_e1 = P.op("DVE", lambda: nc.vector.tensor_copy(
            mean_cols[:, :], PH[0][:, 0:8]), deps=[st_mc[-1][0]] + st_pk_all)
        st_e2 = P.op("DVE", lambda: nc.vector.tensor_copy(
            rstd_cols[:, :], PH[1][:, 0:8]), deps=[st_mc[-1][1]])
        ph_free[0] = st_e1
        ph_free[1] = st_e2
        st_b2c = P.op("DVE", lambda: nc.vector.tensor_mul(
            bias2_cols[:, :], mean_cols[:, :], rstd_cols[:, :]), deps=[st_e1, st_e2])
        st_b2c = P.op("DVE", lambda: nc.vector.tensor_scalar_mul(
            bias2_cols[:, :], bias2_cols[:, :], -1.0), deps=[st_b2c])

        st_out = []
        hiT_free = [st_ysh_all, st_ysh_all]
        loT_free = [st_ysh_all, st_ysh_all]
        rs_free = [st_ysh_all, st_ysh_all]
        os_free = [st_ysh_all, st_ysh_all]
        for j in range(NT):
            jr = j % 2
            srch = r2hi_park.rearrange("c t -> c t")[:, j * 128:(j + 1) * 128]
            stth = P.dma("SPQ", f"cGH{jr}", lambda jr=jr, srch=srch: nc.sync.dma_start(
                out=hiT[jr][:, :], in_=srch, transpose=True),
                deps=st_pk_all + [hiT_free[jr]])
            srcl = r2lo_park[:, j * 128:(j + 1) * 128]
            sttl = P.dma("SPQ", f"cGL{jr}", lambda jr=jr, srcl=srcl: nc.sync.dma_start(
                out=loT[jr][:, :], in_=srcl, transpose=True),
                deps=st_pk_all + [loT_free[jr]])
            st_rs = P.op("DVE", lambda jr=jr: nc.vector.tensor_add(
                rsum[jr][:, :], hiT[jr][:, :], loT[jr][:, :]),
                deps=[stth, sttl, rs_free[jr]])
            hiT_free[jr] = st_rs
            loT_free[jr] = st_rs
            st_nm = P.op("ACT", lambda jr=jr, j=j: nc.scalar.activation(
                out=out_scr[jr][:, :], in_=rsum[jr][:, :], func=AF.Identity,
                bias=bias2_cols[:, j:j + 1], scale=rstd_cols[:, j:j + 1]),
                deps=[st_rs, st_b2c, os_free[jr]])
            rs_free[jr] = st_nm
            dsto = out_own[j * 128:(j + 1) * 128, :]
            sto = P.dma("ACTQ", f"cOUT{jr}", lambda dsto=dsto, jr=jr: nc.scalar.dma_start(
                out=dsto, in_=out_scr[jr][:, :]), deps=[st_nm])
            os_free[jr] = sto
            st_out.append(sto)

        P.op("SP", lambda: nc.sync.nop(), deps=st_out, inc=0)

        if stage is not None:
            cut = stage if isinstance(stage, int) else P.marks[stage]
            P.ops = P.ops[:cut]
            tcounts = {}
            for (_h, _fn, _d, _inc, _sem) in P.ops:
                tcounts[_sem] = tcounts.get(_sem, 0) + _inc
            deps_all = [(kk, vv) for kk, vv in tcounts.items() if vv]
            for j in range(NT):
                dsto = out_own[j * 128:(j + 1) * 128, 0:1024]
                P.dma("ACTQ", "cDBG", lambda dsto=dsto: nc.scalar.dma_start(
                    out=dsto, in_=scr[0][:, :]),
                    deps=(deps_all if j == 0 else []))
            P.op("SP", lambda: nc.sync.nop(), deps=[("cDBG", 16 * NT)], inc=0)

        with contextlib.ExitStack() as sems_stk:
            sem_names = sorted(P.counts.keys())
            sems = {k: sems_stk.enter_context(nc.semaphore("sem_" + k))
                    for k in sem_names}
            block = sems_stk.enter_context(nc.Block())

            def emit_for(host):
                known = {}
                eng = {"PE": nc.tensor, "DVE": nc.vector, "ACT": nc.scalar,
                       "POOL": nc.gpsimd, "SP": nc.sync}[host]
                for (h, fn, deps, inc, sem_name) in P.ops:
                    if h != host:
                        continue
                    for (dn, dv) in deps:
                        if known.get(dn, 0) >= dv:
                            continue
                        eng.wait_ge(sems[dn], dv)
                        known[dn] = dv
                    instr = fn()
                    if inc:
                        instr.then_inc(sems[sem_name], inc)

            @block.tensor
            def _(t):
                emit_for("PE")

            @block.vector
            def _(v):
                emit_for("DVE")

            @block.scalar
            def _(s):
                emit_for("ACT")

            @block.gpsimd
            def _(g):
                emit_for("POOL")

            @block.sync
            def _(s):
                emit_for("SP")

    return nc


_NC_CACHE = None


def _get_nc():
    global _NC_CACHE
    if _NC_CACHE is None:
        _NC_CACHE = build_program()
    return _NC_CACHE


def kernel(L0_attn_x, L0_attn_W, L0_attn_b, L0_ffn_router_weight,
           L0_ffn_sh_gate, L0_ffn_sh_up, L0_ffn_sh_down,
           L0_ffn_r_gate, L0_ffn_r_up, L0_ffn_r_down):
    x = np.ascontiguousarray(np.asarray(L0_attn_x, dtype=np.float32))
    w = np.ascontiguousarray(np.asarray(L0_attn_W, dtype=np.float32))
    bb = np.ascontiguousarray(np.asarray(L0_attn_b, dtype=np.float32))
    rw = np.ascontiguousarray(np.asarray(L0_ffn_router_weight, dtype=np.float32))
    sg = np.ascontiguousarray(np.asarray(L0_ffn_sh_gate, dtype=np.float32))
    su = np.ascontiguousarray(np.asarray(L0_ffn_sh_up, dtype=np.float32))
    sd = np.ascontiguousarray(np.asarray(L0_ffn_sh_down, dtype=np.float32))
    rg = np.ascontiguousarray(np.asarray(L0_ffn_r_gate, dtype=np.float32))
    ru = np.ascontiguousarray(np.asarray(L0_ffn_r_up, dtype=np.float32))
    rd = np.ascontiguousarray(np.asarray(L0_ffn_r_down, dtype=np.float32))

    zeros_warm = np.zeros((WARM, D), dtype=np.float32)
    in_maps = []
    for c in range(NCORE):
        bi, hi = c // 2, c % 2
        xo = np.ascontiguousarray(x[bi, hi * TPC:(hi + 1) * TPC, :])
        xw = zeros_warm if hi == 0 else np.ascontiguousarray(
            x[bi, hi * TPC - WARM: hi * TPC, :])
        in_maps.append({
            "x_own": xo, "x_warm": xw, "w_attn": w, "b_attn": bb,
            "router_w": rw, "sh_gate": sg, "sh_up": su, "sh_down": sd,
            "r_gate": rg, "r_up": ru, "r_down": rd,
        })

    nc = _get_nc()
    res = run_bass_kernel_spmd(nc, in_maps, core_ids=list(range(NCORE)))
    out = np.empty((B, S, D), dtype=np.float32)
    for c in range(NCORE):
        bi, hi = c // 2, c % 2
        out[bi, hi * TPC:(hi + 1) * TPC, :] = res.results[c]["out_own"]
    return out


# version 08
# speedup vs baseline: 1.0300x; 1.0300x over previous
import os
import numpy as np
import ml_dtypes
import contextlib
import concourse.bass as bass
import concourse.mybir as mybir
from concourse.bass_utils import run_bass_kernel_spmd

F32 = mybir.dt.float32
BF16 = mybir.dt.bfloat16
AF = mybir.ActivationFunctionType
ALU = mybir.AluOpType
AXL = mybir.AxisListType

B, S, D = 4, 2048, 2048
NCORE = 8
TPC = 1024
HALF = 512
NT = TPC // 128
ND = D // 128
WARM = 128
INTER = 8192
RINTER = 1024
NSH = INTER // 128
NRES = 56
NR = RINTER // 128
NE = 8


class Prog:

    def __init__(self):
        self.ops = []
        self.marks = {}
        self.counts = {k: 0 for k in ("PE", "DVE", "ACT", "POOL", "SP")}

    def mark(self, name):
        self.marks[name] = len(self.ops)

    def op(self, eng, fn, deps=(), inc=1):
        deps = [d for d in deps if d is not None]
        self.counts[eng] += inc
        self.ops.append((eng, fn, deps, inc, eng))
        return (eng, self.counts[eng]) if inc else None

    def dma(self, queue, chan, fn, deps=()):
        deps = [d for d in deps if d is not None]
        host = {"ACTQ": "ACT", "POOLQ": "POOL", "SPQ": "SP"}[queue]
        self.counts[chan] = self.counts.get(chan, 0) + 16
        self.ops.append((host, fn, deps, 16, chan))
        return (chan, self.counts[chan])


def build_program(stage=None):
    nc = bass.Bass("TRN2", target_bir_lowering=False, debug=False)

    x_own = nc.dram_tensor("x_own", [TPC, D], F32, kind="ExternalInput").ap()
    x_warm = nc.dram_tensor("x_warm", [WARM, D], F32, kind="ExternalInput").ap()
    w_attn = nc.dram_tensor("w_attn", [D, D], F32, kind="ExternalInput").ap()
    b_attn = nc.dram_tensor("b_attn", [D], F32, kind="ExternalInput").ap()
    router_w = nc.dram_tensor("router_w", [NE, D], F32, kind="ExternalInput").ap()
    sh_gate = nc.dram_tensor("sh_gate", [INTER, D], F32, kind="ExternalInput").ap()
    sh_up = nc.dram_tensor("sh_up", [INTER, D], F32, kind="ExternalInput").ap()
    sh_down = nc.dram_tensor("sh_down", [D, INTER], F32, kind="ExternalInput").ap()
    r_gate = nc.dram_tensor("r_gate", [RINTER, D], F32, kind="ExternalInput").ap()
    r_up = nc.dram_tensor("r_up", [RINTER, D], F32, kind="ExternalInput").ap()
    r_down = nc.dram_tensor("r_down", [D, RINTER], F32, kind="ExternalInput").ap()
    out_own = nc.dram_tensor("out_own", [TPC, D], F32, kind="ExternalOutput").ap()

    h1_park = nc.dram_tensor("h1_park", [D, TPC], F32).ap()
    r2_park = nc.dram_tensor("r2_park", [D, TPC], F32).ap()
    yr_park = nc.dram_tensor("yr_park", [RINTER, TPC], BF16).ap()
    ysh_park = nc.dram_tensor("ysh_park", [(NSH - NRES) * 128, TPC], BF16).ap()
    comb_dram = nc.dram_tensor("comb_dram", [1, TPC], F32).ap()
    down_bf = nc.dram_tensor("down_bf", [D, INTER], BF16).ap()
    rdown_bf = nc.dram_tensor("rdown_bf", [D, RINTER], BF16).ap()
    r2hi_park = nc.dram_tensor("r2hi_park", [D, TPC], BF16).ap()
    r2lo_park = nc.dram_tensor("r2lo_park", [D, TPC], BF16).ap()

    ident_f_d = nc.inline_tensor(np.eye(128, dtype=np.float32), "c_identf")
    ident_b_d = nc.inline_tensor(
        np.eye(128, dtype=np.float32).astype(ml_dtypes.bfloat16), "c_identb")
    ones_cb_d = nc.inline_tensor(
        np.ones((128, 1), dtype=np.float32).astype(ml_dtypes.bfloat16), "c_onescb")
    ones_rf_d = nc.inline_tensor(np.ones((1, 128), dtype=np.float32), "c_onesrf")
    eps_d = nc.inline_tensor(np.full((128, 1), 1e-5, dtype=np.float32), "c_eps")

    MAPSIZE = 210688
    _guard = nc.alloc_sbuf_tensor("map_guard", [128, MAPSIZE], mybir.dt.uint8)
    SBUF_BASE = nc.lookup_mloc(_guard).addr
    def at(name, off, shape, dtype):
        return nc.alloc_sbuf_tensor_at(name, shape, dtype, offset=SBUF_BASE + off)

    ident_f = at("ident_f", 0, [128, 128], F32)
    ident_b = at("ident_b", 512, [128, 128], BF16)
    ones_cb = at("ones_cb", 768, [128, 1], BF16)
    bias_cols = at("bias_cols", 800, [128, 16], F32)
    sinit = at("sinit", 864, [128, 16], F32)
    ones_rf = at("ones_rf", 928, [1, 128], F32)
    routerT = at("routerT", 1440, [128, 128], BF16)
    sum_row = at("sum_row", 1696, [1, 1024], F32)
    sq_row = at("sq_row", 5792, [1, 1024], F32)
    wnat = [at(f"wnat{i}", 9888 + i * 8192, [128, 2048], BF16) for i in range(2)]
    wT = [at(f"wT{i}", 13984 + i * 8192, [128, 2048], BF16) for i in range(2)]
    rdn_nat = at("rdn_nat", 26272, [128, 1024], BF16)
    rdnT = at("rdnT", 28320, [128, 1024], BF16)
    r1b_t = at("r1b_t", 30368, [128, 1024], BF16)
    sqb_t = at("sqb_t", 32416, [128, 1024], BF16)
    scr = [at(f"scr{i}", 34464 + i * 4096, [128, 1024], F32) for i in range(6)]
    lo_scr = [at(f"lo_scr{i}", 34464 + 4 * 4096 + i * 2048, [128, 1024], BF16)
              for i in range(2)]
    gub = [at(f"gub{i}", 34464 + i * 4096, [128, 2048], BF16) for i in range(4)]
    silu2 = [at(f"silu2_{i}", 30368 + i * 2048, [128, 512], F32) for i in range(2)]
    silu01 = [at(f"silu{i}", 59040 + i * 2048, [128, 512], F32) for i in range(2)]
    r1b_t2 = at("r1b_t2", 59040, [128, 1024], BF16)
    sqb_t2 = at("sqb_t2", 61088, [128, 1024], BF16)
    RX = 63136
    x_b = at("x_b", RX, [128, ND * 1024], BF16)
    y_r2 = at("y_r2", RX, [128, NR * 1024], BF16)
    yback = [at(f"yback{i}", RX + 16384 + i * 2048, [128, 1024], BF16)
             for i in range(NSH - NRES)]
    RY = 95904
    x_fm = at("x_fm", RY, [128, ND * 1024], F32)
    xw_fm = at("xw_fm", RY + 65536, [128, ND * WARM], F32)
    xw_b = at("xw_b", RY + 73728, [128, ND * WARM], BF16)
    meanB = at("meanB", RY + 77824, [128, 1024], F32)
    rstdB = at("rstdB", RY + 81920, [128, 1024], F32)
    xtm = at("xtm", RY + 86016, [128, 2048], F32)
    xtm2 = at("xtm2", RY + 77824, [128, 2048], F32)
    wnat2 = at("wnat2", RY + 86016, [128, 2048], BF16)
    wT2 = at("wT2", RY + 90112, [128, 2048], BF16)
    y_r = at("y_r", RY, [128, NR * 1024], BF16)
    comb_bcast = at("comb_bcast", RY + 16384, [128, 1024], F32)
    logits_b = at("logits_b", RY + 20480, [8, 1024], BF16)
    lg_scr = at("lg_scr", RY + 22528, [128, 8], F32)
    eq_scr = at("eq_scr", RY + 22560, [128, 8], F32)
    m1c = at("m1c", RY + 22592, [128, 1], F32)
    m2c = at("m2c", RY + 22624, [128, 1], F32)
    cntc = at("cntc", RY + 22656, [128, 1], F32)
    c2c = at("c2c", RY + 22688, [128, 1], F32)
    v2c = at("v2c", RY + 22720, [128, 1], F32)
    e2c = at("e2c", RY + 22752, [128, 1], F32)
    rzc = at("rzc", RY + 22784, [128, 1], F32)
    comb_cols = at("comb_cols", RY + 22816, [128, 8], F32)
    comb_row = at("comb_row", RY + 22848, [1, 1024], F32)
    y_sh = at("y_sh", RY, [128, NRES * 1024], BF16)
    hiT = [at(f"hiT{i}", RY + i * 4096, [128, 2048], BF16) for i in range(2)]
    loT = [at(f"loT{i}", RY + 8192 + i * 4096, [128, 2048], BF16) for i in range(2)]
    rsum = [at(f"rsum{i}", RY + 16384 + i * 8192, [128, 2048], F32) for i in range(2)]
    out_scr = [at(f"out_scr{i}", RY + 32768 + i * 8192, [128, 2048], F32)
               for i in range(2)]
    mean_cols = at("mean_cols", RY + 49152, [128, 8], F32)
    rstd_cols = at("rstd_cols", RY + 49184, [128, 8], F32)
    bias2_cols = at("bias2_cols", RY + 49216, [128, 8], F32)
    eps_c = at("eps_c", 210624, [128, 1], F32)
    nc.const_aps.register(F32, 1e-5, eps_c.ap()[:, :])

    P = Prog()
    with contextlib.ExitStack() as stk:
        PT = stk.enter_context(nc.psum_tensor("PT", [128, 2048], F32))
        PH = [stk.enter_context(nc.psum_tensor(f"PH{i}", [128, 512], F32))
              for i in range(3)]
        PBt = stk.enter_context(nc.psum_tensor("PBt", [128, 1024], BF16))

        ph_free = [None, None, None]
        ph_ctr = [0]

        def next_ph():
            i = ph_ctr[0] % 3
            ph_ctr[0] += 1
            return i

        P.dma("ACTQ", "cCONST", lambda: nc.scalar.dma_start(out=ident_f[:, :], in_=ident_f_d.ap()[:, :]))
        P.dma("ACTQ", "cCONST", lambda: nc.scalar.dma_start(out=ident_b[:, :], in_=ident_b_d.ap()[:, :]))
        P.dma("ACTQ", "cCONST", lambda: nc.scalar.dma_start(out=ones_cb[:, :], in_=ones_cb_d.ap()[:, :]))
        P.dma("ACTQ", "cCONST", lambda: nc.scalar.dma_start(out=ones_rf[:, :], in_=ones_rf_d.ap()[:, :]))
        P.dma("ACTQ", "cCONST", lambda: nc.scalar.dma_start(out=eps_c[:, :], in_=eps_d.ap()[:, :]))
        for k in range(ND):
            dst = bias_cols[:, k:k + 1]
            src = b_attn[k * 128:(k + 1) * 128].rearrange("(p o) -> p o", o=1)
            st_consts = P.dma("ACTQ", "cCONST", lambda dst=dst, src=src: nc.scalar.dma_start(out=dst, in_=src))
        st_if = st_ib = st_oc = st_or = st_eps = st_bias = st_consts

        def tr_in(st_src, src, fm_dst_fn, bf_dst_fn, extra=(), prev_evs=None):
            trs, evs, cvs = [], [], []
            for dc in range(ND):
                po = PT[:, dc * 128:(dc + 1) * 128]
                si = src[:, dc * 128:(dc + 1) * 128]
                bank_war = [prev_evs[(dc // 4) * 4 + 3]] if prev_evs else []
                trs.append(P.op("PE", lambda po=po, si=si: nc.tensor.transpose(
                    po, si, ident_f[:, :]), deps=[st_src, st_if] + list(extra) + bank_war))
            for dc in range(ND):
                po = PT[:, dc * 128:(dc + 1) * 128]
                fd = fm_dst_fn(dc)
                evs.append(P.op("DVE", lambda fd=fd, po=po: nc.vector.tensor_copy(fd, po),
                                deps=[trs[dc | 3]]))
                bd = bf_dst_fn(dc)
                cvs.append(P.op("ACT", lambda bd=bd, fd=fd: nc.scalar.copy(bd, fd),
                                deps=[evs[dc]]))
            return trs, evs, cvs

        st_xw_l = P.dma("ACTQ", "cXTM", lambda: nc.scalar.dma_start(out=xtm[:, :], in_=x_warm[:, :]))
        xw_tr, xw_ev, xw_cv = tr_in(
            st_xw_l, xtm,
            lambda dc: xw_fm[:, dc * WARM:(dc + 1) * WARM],
            lambda dc: xw_b[:, dc * WARM:(dc + 1) * WARM])
        x_tr_last = xw_tr[-1]
        prev_evac = xw_ev
        prev_read = [xw_tr[-1], None]
        xtms = [xtm, xtm2]
        x_ev_all, x_cv_all = [], []
        for tt in range(NT):
            xt = xtms[tt % 2]
            src = x_own[tt * 128:(tt + 1) * 128, :]
            stl = P.dma("ACTQ", f"cXTM{tt % 2}", lambda src=src, xt=xt: nc.scalar.dma_start(out=xt[:, :], in_=src),
                        deps=[prev_read[tt % 2]])
            trs, evs, cvs = tr_in(
                stl, xt,
                lambda dc, tt=tt: x_fm[:, dc * 1024 + tt * 128: dc * 1024 + tt * 128 + 128],
                lambda dc, tt=tt: x_b[:, dc * 1024 + tt * 128: dc * 1024 + tt * 128 + 128],
                extra=[x_tr_last], prev_evs=prev_evac)
            prev_evac = evs
            x_tr_last = trs[-1]
            prev_read[tt % 2] = trs[-1]
            x_ev_all += evs
            x_cv_all += cvs
        P.mark("A")
        st_x_fm = x_ev_all[-1]
        st_x_b = x_cv_all[-1]
        st_xw_fm = xw_ev[-1]
        st_xw_b = xw_cv[-1]
        st_pt_clear = x_ev_all[-1]

        wring_nat = [wnat[0], wnat[1], wnat2]
        wring_T = [wT[0], wT[1], wT2]
        wnat_free = [None, None, x_tr_last]
        wT_free = [None, None, x_tr_last]

        def stream_strip(src_ap, ri, nblocks=16):
            nat, wTt = wring_nat[ri], wring_T[ri]
            cols = nblocks * 128
            stc = P.dma("POOLQ", f"cN{ri}", lambda nat=nat, src_ap=src_ap, cols=cols:
                        nc.gpsimd.dma_start(out=nat[:, 0:cols], in_=src_ap),
                        deps=[wnat_free[ri]])
            dst = wTt.ap()[:, 0:cols].rearrange("p (c i) -> p c i", c=nblocks)
            stT = P.dma("SPQ", f"cT{ri}", lambda dst=dst, nat=nat, cols=cols: nc.sync.dma_start(
                out=dst, in_=nat[:, 0:cols], transpose=True),
                deps=[stc, wT_free[ri]])
            wnat_free[ri] = stT
            return stT, wTt

        st_ppre = {}
        st_ppre_r = {}

        def emit_ppre(kd, quarters=range(4)):
            for q in quarters:
                srcw = sh_down[kd * 128:(kd + 1) * 128, q * 2048:(q + 1) * 2048]
                dstw = down_bf[kd * 128:(kd + 1) * 128, q * 2048:(q + 1) * 2048]
                st_ppre[kd] = P.dma("POOLQ", f"cPP{kd}", lambda dstw=dstw, srcw=srcw:
                                    nc.gpsimd.dma_start(out=dstw, in_=srcw))

        def emit_ppre_r(half, quarters=range(4)):
            for q in quarters:
                srcw = r_down[half * 1024 + q * 256: half * 1024 + (q + 1) * 256, :]
                dstw = rdown_bf[half * 1024 + q * 256: half * 1024 + (q + 1) * 256, :]
                st_ppre_r[half] = P.dma("POOLQ", f"cPPR{half}", lambda dstw=dstw, srcw=srcw:
                                        nc.gpsimd.dma_start(out=dstw, in_=srcw))

        lam = [scr[0], scr[1]]
        bsc = [scr[2], scr[3]]
        ysc = [scr[4], scr[5]]
        lam_free = [None, None]
        bsc_free = [None, None]
        ysc_free = [None, None]
        st_r1 = [None] * ND
        st_stats_pe = None
        pending_stats = None
        statsb = [(r1b_t, sqb_t), (r1b_t2, sqb_t2)]
        statsb_free = [None, None]

        def emit_stats_stage(k, st_src, src_buf=None):
            rb, qb = statsb[k % 2]
            sb_src = (lambda: x_fm[:, k * 1024:(k + 1) * 1024]) if src_buf is None else src_buf
            st_conv = P.op("ACT", lambda rb=rb, sb_src=sb_src: nc.scalar.copy(
                rb[:, :], sb_src()), deps=[st_src, statsb_free[k % 2]])
            st_sq = P.op("ACT", lambda qb=qb, sb_src=sb_src: nc.scalar.activation(
                out=qb[:, :], in_=sb_src(), func=AF.Square, scale=1.0),
                deps=[st_src, statsb_free[k % 2]])
            last = None
            for h in range(2):
                po = PT[0:1, h * 512:(h + 1) * 512]
                rh = rb[:, h * HALF:(h + 1) * HALF]
                last = P.op("PE", lambda po=po, rh=rh, k=k: nc.tensor.matmul(
                    po, ones_cb[:, :], rh, start=(k == 0), stop=(k == ND - 1)),
                    deps=[st_conv, st_oc] + ([st_pt_clear] if k == 0 else []), inc=0)
                po2 = PT[0:1, 1024 + h * 512: 1536 + h * 512]
                rh2 = qb[:, h * HALF:(h + 1) * HALF]
                last = P.op("PE", lambda po2=po2, rh2=rh2, k=k: nc.tensor.matmul(
                    po2, ones_cb[:, :], rh2, start=(k == 0), stop=(k == ND - 1)),
                    deps=[st_sq], inc=(1 if h == 1 else 0))
            statsb_free[k % 2] = last
            return last

        for k in range(ND):
            r = k % 3
            p2 = k % 2
            stT, wTt = stream_strip(w_attn[k * 128:(k + 1) * 128, :], r)
            lam_t, b_t, ys_t = lam[p2], bsc[p2], ysc[p2]

            zw = next_ph()
            mm = None
            for dc in range(ND):
                lhs = wTt[:, dc * 128:(dc + 1) * 128]
                rhs = xw_b[:, dc * WARM:(dc + 1) * WARM]
                mm = P.op("PE", lambda lhs=lhs, rhs=rhs, zw=zw, dc=dc: nc.tensor.matmul(
                    PH[zw][:, 0:WARM], lhs, rhs, start=(dc == 0), stop=(dc == ND - 1)),
                    deps=([stT, st_xw_b, ph_free[zw]] if dc == 0 else []),
                    inc=(1 if dc == ND - 1 else 0))
            st_sigw = P.op("ACT", lambda zw=zw, k=k, lam_t=lam_t: nc.scalar.activation(
                out=lam_t[:, 0:WARM], in_=PH[zw][:, 0:WARM], func=AF.Sigmoid,
                bias=bias_cols[:, k:k + 1], scale=1.0),
                deps=[mm, st_bias, lam_free[p2]])
            ph_free[zw] = st_sigw
            st_b1 = P.op("DVE", lambda lam_t=lam_t, b_t=b_t: nc.vector.tensor_scalar(
                out=b_t[:, 0:WARM], in0=lam_t[:, 0:WARM], scalar1=-1.0, scalar2=1.0,
                op0=ALU.mult, op1=ALU.add), deps=[st_sigw, bsc_free[p2]])
            st_b2 = P.op("DVE", lambda b_t=b_t, k=k: nc.vector.tensor_mul(
                b_t[:, 0:WARM], b_t[:, 0:WARM], xw_fm[:, k * WARM:(k + 1) * WARM]),
                deps=[st_b1, st_xw_fm])
            st_scw = P.op("DVE", lambda lam_t=lam_t, b_t=b_t, ys_t=ys_t:
                          nc.vector.tensor_tensor_scan(
                out=ys_t[:, 0:WARM], data0=lam_t[:, 0:WARM], data1=b_t[:, 0:WARM],
                initial=0.0, op0=ALU.mult, op1=ALU.add),
                deps=[st_b2, ysc_free[p2]])
            st_si = P.op("DVE", lambda ys_t=ys_t, k=k: nc.vector.tensor_copy(
                sinit[:, k:k + 1], ys_t[:, WARM - 1:WARM]), deps=[st_scw])

            st_sig = [None, None]
            for h in range(2):
                zi = next_ph()
                mm = None
                for dc in range(ND):
                    lhs = wTt[:, dc * 128:(dc + 1) * 128]
                    rhs = x_b[:, dc * 1024 + h * HALF: dc * 1024 + (h + 1) * HALF]
                    mm = P.op("PE", lambda lhs=lhs, rhs=rhs, zi=zi, dc=dc: nc.tensor.matmul(
                        PH[zi][:, :], lhs, rhs, start=(dc == 0), stop=(dc == ND - 1)),
                        deps=([stT, st_x_b, ph_free[zi]] if dc == 0 else []),
                        inc=(1 if dc == ND - 1 else 0))
                if h == 1:
                    wT_free[r] = mm
                st_sig[h] = P.op("ACT", lambda zi=zi, h=h, k=k, lam_t=lam_t:
                                 nc.scalar.activation(
                    out=lam_t[:, h * HALF:(h + 1) * HALF], in_=PH[zi][:, :],
                    func=AF.Sigmoid, bias=bias_cols[:, k:k + 1], scale=1.0),
                    deps=[mm, st_scw])
                ph_free[zi] = st_sig[h]
            st_b1 = P.op("DVE", lambda lam_t=lam_t, b_t=b_t: nc.vector.tensor_scalar(
                out=b_t[:, :], in0=lam_t[:, :], scalar1=-1.0, scalar2=1.0,
                op0=ALU.mult, op1=ALU.add), deps=[st_sig[0], st_sig[1], st_scw])
            st_b2 = P.op("DVE", lambda b_t=b_t, k=k: nc.vector.tensor_mul(
                b_t[:, :], b_t[:, :], x_fm[:, k * 1024:(k + 1) * 1024]),
                deps=[st_b1, st_x_fm])
            st_scan = P.op("DVE", lambda lam_t=lam_t, b_t=b_t, ys_t=ys_t, k=k:
                           nc.vector.tensor_tensor_scan(
                out=ys_t[:, :], data0=lam_t[:, :], data1=b_t[:, :],
                initial=sinit[:, k:k + 1], op0=ALU.mult, op1=ALU.add),
                deps=[st_b2, st_si])
            lam_free[p2] = st_scan
            bsc_free[p2] = st_scan
            st_r1[k] = P.op("DVE", lambda ys_t=ys_t, k=k: nc.vector.tensor_add(
                x_fm[:, k * 1024:(k + 1) * 1024],
                x_fm[:, k * 1024:(k + 1) * 1024], ys_t[:, :]), deps=[st_scan])
            ysc_free[p2] = st_r1[k]

            if pending_stats is not None:
                st_stats_pe = emit_stats_stage(*pending_stats)
            pending_stats = (k, st_r1[k])
            if k in (6, 8, 10, 12, 14):
                emit_ppre((k - 6) // 2)

        if pending_stats is not None:
            st_stats_pe = emit_stats_stage(*pending_stats)
            pending_stats = None
        st_rw = P.dma("POOLQ", "cN0", lambda: nc.gpsimd.dma_start(
            out=wnat[0][0:8, :], in_=router_w[:, :]), deps=[wnat_free[0]])
        wnat_free[0] = st_rw
        P.mark("C")
        st_sumr = P.op("DVE", lambda: nc.vector.tensor_copy(
            sum_row[:, :], PT[0:1, 0:1024]), deps=[st_stats_pe])
        st_sqr = P.op("DVE", lambda: nc.vector.tensor_copy(
            sq_row[:, :], PT[0:1, 1024:2048]), deps=[st_stats_pe])
        st_mean = P.op("DVE", lambda: nc.vector.tensor_scalar_mul(
            sum_row[:, :], sum_row[:, :], 1.0 / D), deps=[st_sumr])
        st_msq = P.op("DVE", lambda: nc.vector.tensor_scalar_mul(
            sq_row[:, :], sq_row[:, :], 1.0 / D), deps=[st_sqr])
        st_m2 = P.op("DVE", lambda: nc.vector.tensor_tensor(
            scr[0][0:1, :], sum_row[:, :], sum_row[:, :], op=ALU.mult),
            deps=[st_mean, lam_free[0]])
        st_var = P.op("DVE", lambda: nc.vector.tensor_sub(
            sq_row[:, :], sq_row[:, :], scr[0][0:1, :]), deps=[st_m2, st_msq])
        st_std = P.op("ACT", lambda: nc.scalar.activation(
            out=sq_row[:, :], in_=sq_row[:, :], func=AF.Sqrt, bias=1e-5, scale=1.0),
            deps=[st_var, st_eps])
        st_rstd = P.op("DVE", lambda: nc.vector.reciprocal(
            sq_row[:, :], sq_row[:, :]), deps=[st_std])
        st_bc = {}
        for nm, row_ap, dep, dstb in (("m", sum_row, st_mean, meanB),
                                      ("r", sq_row, st_rstd, rstdB)):
            for h in range(2):
                pi = next_ph()
                srow = row_ap[0:1, h * HALF:(h + 1) * HALF]
                mm = P.op("PE", lambda pi=pi, srow=srow: nc.tensor.matmul(
                    PH[pi][:, :], ones_rf[:, :], srow, start=True, stop=True),
                    deps=[dep, st_or, ph_free[pi]])
                dst = dstb[:, h * HALF:(h + 1) * HALF]
                ev = P.op("DVE", lambda dst=dst, pi=pi: nc.vector.tensor_copy(
                    dst, PH[pi][:, :]), deps=[mm])
                ph_free[pi] = ev
                st_bc[(nm, h)] = ev
        st_meanB = st_bc[("m", 1)]
        st_rstdB = st_bc[("r", 1)]
        nrm = scr[1]
        st_nrm_free = lam_free[1]
        st_h1 = [None] * ND
        st_h1b = [None] * ND
        st_h1pk = [None] * ND
        for k in range(ND):
            st_s1 = P.op("DVE", lambda k=k: nc.vector.tensor_sub(
                nrm[:, :], x_fm[:, k * 1024:(k + 1) * 1024], meanB[:, :]),
                deps=[st_r1[k], st_meanB, st_bc[("m", 0)], st_nrm_free])
            st_h1[k] = P.op("DVE", lambda k=k: nc.vector.tensor_mul(
                x_fm[:, k * 1024:(k + 1) * 1024], nrm[:, :], rstdB[:, :]),
                deps=[st_s1, st_rstdB, st_bc[("r", 0)]])
            st_nrm_free = st_h1[k]
            st_h1b[k] = P.op("ACT", lambda k=k: nc.scalar.copy(
                x_b[:, k * 1024:(k + 1) * 1024], x_fm[:, k * 1024:(k + 1) * 1024]),
                deps=[st_h1[k], st_x_b])
            dst = h1_park[k * 128:(k + 1) * 128, :]
            st_h1pk[k] = P.dma("ACTQ", "cH1P", lambda k=k, dst=dst: nc.scalar.dma_start(
                out=dst, in_=x_fm[:, k * 1024:(k + 1) * 1024]), deps=[st_h1[k]])
            if k in (5, 10, 15):
                emit_ppre(5 + k // 5 - 1)
        st_h1b_all = st_h1b[-1]
        st_h1_parked = st_h1pk[-1]

        P.mark("D")
        st_rTt = []
        for dc in range(ND):
            po = PBt[:, dc * 8:(dc + 1) * 8]
            si = wnat[0][0:8, dc * 128:(dc + 1) * 128]
            st_rTt.append(P.op("PE", lambda po=po, si=si: nc.tensor.transpose(
                po, si, ident_b[0:8, 0:8]), deps=[st_rw, st_ib]))
        st_rT = []
        for dc in range(ND):
            po = PBt[:, dc * 8:(dc + 1) * 8]
            st_rT.append(P.op("ACT", lambda dc=dc, po=po: nc.scalar.copy(
                routerT[:, dc * 8:(dc + 1) * 8], po), deps=[st_rTt[-1]]))
        wnat_free[0] = st_rT[-1]
        pb_free = st_rT[-1]
        st_rT_all = st_rT[-1]
        st_lg = [None, None]
        for h in range(2):
            pi = next_ph()
            mm = None
            for dc in range(ND):
                lhs = routerT[:, dc * 8:(dc + 1) * 8]
                rhs = x_b[:, dc * 1024 + h * HALF: dc * 1024 + (h + 1) * HALF]
                mm = P.op("PE", lambda lhs=lhs, rhs=rhs, pi=pi, dc=dc: nc.tensor.matmul(
                    PH[pi][0:8, :], lhs, rhs, start=(dc == 0), stop=(dc == ND - 1)),
                    deps=([st_rT_all, st_h1b_all, ph_free[pi]] if dc == 0 else []),
                    inc=(1 if dc == ND - 1 else 0))
            st_lg[h] = P.op("ACT", lambda h=h, pi=pi: nc.scalar.copy(
                logits_b[0:8, h * HALF:(h + 1) * HALF], PH[pi][0:8, :]),
                deps=[mm, st_h1_parked])
            ph_free[pi] = st_lg[h]
        silu_free = [[st_stats_pe, st_stats_pe], [st_stats_pe, st_stats_pe]]

        def gu_strips(wg, wu, nstrips, ydst_fn, scale, scale_dep, ydep_fn,
                      park_fn=None, order=None, hook=None, rings=None):
            (gnat, gnatF, gT, gTF, unat, unatF, uT, uTF, tag) = rings
            depth = len(gnat)

            def flat(x):
                if x is None:
                    return []
                if isinstance(x, tuple) and len(x) == 2 and isinstance(x[0], str):
                    return [x]
                return [d for d in x if d is not None]

            st_y_all = []
            for si, s in enumerate(order if order is not None else range(nstrips)):
                if hook is not None and si in hook:
                    hook[si]()
                ydeps = ydep_fn(s)
                ri = si % depth
                natg, wTg = gnat[ri], gT[ri]
                sg = wg[s * 128:(s + 1) * 128, :]
                stcg = P.dma("POOLQ", f"c{tag}gN{ri}", lambda natg=natg, sg=sg:
                             nc.gpsimd.dma_start(out=natg[:, :], in_=sg),
                             deps=flat(gnatF[ri]))
                dstg = wTg.ap().rearrange("p (c i) -> p c i", c=16)
                stTg = P.dma("SPQ", f"c{tag}gT{ri}", lambda dstg=dstg, natg=natg:
                             nc.sync.dma_start(out=dstg, in_=natg[:, :], transpose=True),
                             deps=[stcg] + flat(gTF[ri]))
                gnatF[ri] = stTg
                natu, wTu = unat[ri], uT[ri]
                su = wu[s * 128:(s + 1) * 128, :]
                stcu = P.dma("POOLQ", f"c{tag}uN{ri}", lambda natu=natu, su=su:
                             nc.gpsimd.dma_start(out=natu[:, :], in_=su),
                             deps=flat(unatF[ri]))
                dstu = wTu.ap().rearrange("p (c i) -> p c i", c=16)
                stTu = P.dma("SPQ", f"c{tag}uT{ri}", lambda dstu=dstu, natu=natu:
                             nc.sync.dma_start(out=dstu, in_=natu[:, :], transpose=True),
                             deps=[stcu] + flat(uTF[ri]))
                unatF[ri] = stTu
                for h in range(2):
                    gi = next_ph()
                    mmg = None
                    for dc in range(ND):
                        lhs = wTg[:, dc * 128:(dc + 1) * 128]
                        rhs = x_b[:, dc * 1024 + h * HALF: dc * 1024 + (h + 1) * HALF]
                        mmg = P.op("PE", lambda lhs=lhs, rhs=rhs, gi=gi, dc=dc:
                                   nc.tensor.matmul(
                            PH[gi][:, :], lhs, rhs, start=(dc == 0), stop=(dc == ND - 1)),
                            deps=([stTg, st_h1b_all, ph_free[gi]] if dc == 0 else []),
                            inc=(1 if dc == ND - 1 else 0))
                    ui = next_ph()
                    mmu = None
                    for dc in range(ND):
                        lhs = wTu[:, dc * 128:(dc + 1) * 128]
                        rhs = x_b[:, dc * 1024 + h * HALF: dc * 1024 + (h + 1) * HALF]
                        mmu = P.op("PE", lambda lhs=lhs, rhs=rhs, ui=ui, dc=dc:
                                   nc.tensor.matmul(
                            PH[ui][:, :], lhs, rhs, start=(dc == 0), stop=(dc == ND - 1)),
                            deps=([stTu, ph_free[ui]] if dc == 0 else []),
                            inc=(1 if dc == ND - 1 else 0))
                    if h == 1:
                        gTF[ri] = mmg
                        uTF[ri] = mmu
                    sscr = silu01[h] if si % 2 == 0 else silu2[h]
                    st_s = P.op("ACT", lambda gi=gi, sscr=sscr: nc.scalar.activation(
                        out=sscr[:, :], in_=PH[gi][:, :], func=AF.Silu, scale=1.0),
                        deps=[mmg, silu_free[h][si % 2]])
                    ph_free[gi] = st_s
                    ydst = ydst_fn(s, h)
                    if scale is None:
                        st_y = P.op("DVE", lambda ydst=ydst, sscr=sscr, ui=ui:
                                    nc.vector.tensor_tensor(
                            ydst, sscr[:, :], PH[ui][:, :], op=ALU.mult),
                            deps=[st_s, mmu] + list(ydeps))
                    else:
                        st_y0 = P.op("DVE", lambda sscr=sscr, ui=ui:
                                     nc.vector.tensor_tensor(
                            sscr[:, :], sscr[:, :], PH[ui][:, :], op=ALU.mult),
                            deps=[st_s, mmu])
                        sc_ap = scale[:, h * HALF:(h + 1) * HALF]
                        st_y = P.op("DVE", lambda ydst=ydst, sscr=sscr, sc_ap=sc_ap:
                                    nc.vector.tensor_mul(ydst, sscr[:, :], sc_ap),
                                    deps=[st_y0, scale_dep] + list(ydeps))
                    ph_free[ui] = st_y
                    silu_free[h][si % 2] = st_y
                    st_y_all.append(st_y)
                if park_fn is not None:
                    park_fn(s, st_y_all[-2], st_y_all[-1])
            return st_y_all

        def e0_ppre(s, a, b):
            if s in (2, 4, 6):
                emit_ppre(8 + s // 2 - 1)

        gu_rings = (
            [wnat[0], gub[0]], [wnat_free[0], (lam_free[0], st_var)],
            [wT[0], gub[1]], [wT_free[0], st_nrm_free],
            [wnat[1], gub[2]], [wnat_free[1], bsc_free[0]],
            [wT[1], gub[3]], [wT_free[1], bsc_free[1]], "G")
        st_yr_unscaled = gu_strips(
            r_gate, r_up, NR,
            lambda s, h: y_r[:, s * 1024 + h * HALF: s * 1024 + (h + 1) * HALF],
            None, None, lambda s: [st_h1_parked], park_fn=e0_ppre,
            rings=gu_rings)

        st_lgT = []
        for j in range(NT):
            po = PBt[:, j * 8:(j + 1) * 8]
            si = logits_b[0:8, j * 128:(j + 1) * 128]
            st_lgT.append(P.op("PE", lambda po=po, si=si: nc.tensor.transpose(
                po, si, ident_b[0:8, 0:8]), deps=[st_lg[0], st_lg[1], pb_free]))
        st_cc = []
        cr_box = [None]

        def emit_comb_j(j):
                po = PBt[:, j * 8:(j + 1) * 8]
                s0 = P.op("DVE", lambda po=po: nc.vector.tensor_copy(lg_scr[:, :], po),
                          deps=[st_lgT[-1], (st_cc[-1] if st_cc else st_h1_parked)])
                s1 = P.op("DVE", lambda: nc.vector.reduce_max(
                    m1c[:, :], lg_scr[:, :], axis=AXL.X), deps=[s0])
                s2 = P.op("DVE", lambda: nc.vector.tensor_scalar(
                    out=eq_scr[:, :], in0=lg_scr[:, :], scalar1=m1c[:, 0:1],
                    scalar2=None, op0=ALU.is_equal), deps=[s1])
                s3 = P.op("DVE", lambda: nc.vector.tensor_reduce(
                    cntc[:, :], eq_scr[:, :], axis=AXL.X, op=ALU.add), deps=[s2])
                s4 = P.op("DVE", lambda: nc.vector.scalar_tensor_tensor(
                    out=eq_scr[:, :], in0=eq_scr[:, :], scalar=-1e30, in1=lg_scr[:, :],
                    op0=ALU.mult, op1=ALU.add), deps=[s3])
                s5 = P.op("DVE", lambda: nc.vector.reduce_max(
                    m2c[:, :], eq_scr[:, :], axis=AXL.X), deps=[s4])
                s6 = P.op("DVE", lambda: nc.vector.tensor_scalar(
                    out=c2c[:, :], in0=cntc[:, :], scalar1=2.0, scalar2=None,
                    op0=ALU.is_ge), deps=[s3])
                s7 = P.op("DVE", lambda: nc.vector.tensor_sub(
                    v2c[:, :], m1c[:, :], m2c[:, :]), deps=[s5])
                s8 = P.op("DVE", lambda: nc.vector.tensor_mul(
                    v2c[:, :], v2c[:, :], c2c[:, :]), deps=[s7, s6])
                s9 = P.op("DVE", lambda: nc.vector.tensor_add(
                    v2c[:, :], v2c[:, :], m2c[:, :]), deps=[s8])
                s10 = P.op("DVE", lambda: nc.vector.tensor_scalar_mul(
                    m1c[:, :], m1c[:, :], -1.0), deps=[s9, s2])
                s11 = P.op("ACT", lambda: nc.scalar.activation(
                    out=e2c[:, :], in_=v2c[:, :], func=AF.Exp, bias=m1c[:, 0:1],
                    scale=1.0), deps=[s10])
                s12 = P.op("DVE", lambda: nc.vector.tensor_scalar_add(
                    rzc[:, :], e2c[:, :], 1.0), deps=[s11])
                s13 = P.op("DVE", lambda: nc.vector.reciprocal(
                    rzc[:, :], rzc[:, :]), deps=[s12])
                s14 = P.op("DVE", lambda j=j: nc.vector.scalar_tensor_tensor(
                    out=comb_cols[:, j:j + 1], in0=e2c[:, :], scalar=rzc[:, 0:1],
                    in1=rzc[:, :], op0=ALU.mult, op1=ALU.add), deps=[s13])
                st_cc.append(s14)

        def emit_comb_tail():
            st_cd = []
            for j in range(NT):
                dst = comb_dram[0:1, j * 128:(j + 1) * 128]
                st_cd.append(P.dma("ACTQ", "cCMB", lambda j=j, dst=dst: nc.scalar.dma_start(
                    out=dst, in_=comb_cols[:, j:j + 1]), deps=[st_cc[j]]))
            cr_box[0] = P.dma("ACTQ", "cCMB", lambda: nc.scalar.dma_start(
                out=comb_row[:, :], in_=comb_dram[:, :]), deps=[st_cd[-1]])
        yr_parked_box = [None]

        def finish_comb():
            st_cr = cr_box[0]
            st_comb = None
            for h in range(2):
                pi = next_ph()
                srow = comb_row[0:1, h * HALF:(h + 1) * HALF]
                mm = P.op("PE", lambda pi=pi, srow=srow: nc.tensor.matmul(
                    PH[pi][:, :], ones_rf[:, :], srow, start=True, stop=True),
                    deps=[st_cr, ph_free[pi]])
                dst = comb_bcast[:, h * HALF:(h + 1) * HALF]
                st_comb = P.op("DVE", lambda dst=dst, pi=pi: nc.vector.tensor_copy(
                    dst, PH[pi][:, :]), deps=[mm])
                ph_free[pi] = st_comb
            st_yr_park = [None] * NR
            for s in range(NR):
                for h in range(2):
                    ya = y_r[:, s * 1024 + h * HALF: s * 1024 + (h + 1) * HALF]
                    ca = comb_bcast[:, h * HALF:(h + 1) * HALF]
                    stsc = P.op("DVE", lambda ya=ya, ca=ca: nc.vector.tensor_mul(
                        ya, ya, ca), deps=[st_yr_unscaled[2 * s + h], st_comb])
                    if h == 1:
                        stsc1 = stsc
                    else:
                        stsc0 = stsc
                dst = yr_park[s * 128:(s + 1) * 128, :]
                srcp = y_r[:, s * 1024:(s + 1) * 1024]
                st_yr_park[s] = P.dma("ACTQ", "cYRP", lambda dst=dst, srcp=srcp:
                                      nc.scalar.dma_start(out=dst, in_=srcp),
                                      deps=[stsc0, stsc1])
            yr_parked_box[0] = st_yr_park[-1]

        P.mark("E0")
        park_stage = [rdn_nat, rdnT]
        park_stage_free = [None, None]
        st_ysh_park = {}

        def ysh_dst(s, h):
            if s < NRES:
                return y_sh[:, s * 1024 + h * HALF: s * 1024 + (h + 1) * HALF]
            return park_stage[s % 2][:, h * HALF:(h + 1) * HALF]

        ppre_pieces = ([(kd, q) for kd in range(11, ND) for q in range(4)] +
                       [("r", hq) for hq in range(8)])

        def park_ysh(s, sty0, sty1):
            if 16 <= s < 16 + len(ppre_pieces):
                kd, q = ppre_pieces[s - 16]
                if kd == "r":
                    emit_ppre_r(q // 4, quarters=[q % 4])
                else:
                    emit_ppre(kd, quarters=[q])
            if s < NRES:
                return
            t = park_stage[s % 2]
            dst = ysh_park[(s - NRES) * 128:(s - NRES + 1) * 128, :]
            st = P.dma("ACTQ", f"cYSP{s % 2}", lambda dst=dst, t=t: nc.scalar.dma_start(
                out=dst, in_=t[:, :]), deps=[sty0, sty1])
            st_ysh_park[s] = st
            park_stage_free[s % 2] = st

        e1_order = list(range(14, NSH)) + list(range(14))
        st_ysh = gu_strips(
            sh_gate, sh_up, NSH, ysh_dst, None, None,
            lambda s: [st_h1_parked] +
                      ([yr_parked_box[0]] if s < 14 else []) +
                      ([park_stage_free[s % 2]] if s >= NRES else []),
            park_fn=park_ysh, order=e1_order,
            hook={**{1 + j: (lambda j=j: emit_comb_j(j)) for j in range(NT)},
                  10: emit_comb_tail, 12: finish_comb},
            rings=gu_rings)
        st_yr_parked = yr_parked_box[0]
        st_ysh_all = st_ysh[-1]
        st_ysh_parked_all = [st_ysh_park[NSH - 2], st_ysh_park[NSH - 1]]
        st_xb_done = st_ysh_all

        P.mark("E1")
        st_yrb = []
        for s in range(NR):
            dst = y_r2[:, s * 1024:(s + 1) * 1024]
            src = yr_park[s * 128:(s + 1) * 128, :]
            st_yrb.append(P.dma("ACTQ", "cYR2", lambda dst=dst, src=src: nc.scalar.dma_start(
                out=dst, in_=src), deps=[st_yr_parked, st_xb_done]))
        st_yrb_all = st_yrb[-1]

        h1j = [bsc[0], bsc[1]]
        r2s = [lam[0], lam[1]]
        (g_gnat, g_gnatF, g_gT, g_gTF, g_unat, g_unatF, g_uT, g_uTF, _tag) = gu_rings
        h1j_free = [g_unatF[1], g_uTF[1]]
        r2s_free = [(st_var, g_gnatF[1]), (st_nrm_free, g_gTF[1])]
        yb_free = [st_xb_done] * (NSH - NRES)
        st_stats2_pe = None
        st_r2pk = [None] * ND
        rdn_free = [park_stage_free[0], park_stage_free[1]]
        statsb_free = [(statsb_free[0], silu_free[0][1], silu_free[1][1]),
                       (silu_free[0][0], silu_free[1][0])]
        lo_free = [ysc_free[0], ysc_free[0]]
        pending_stats2 = None

        def emit_stats2_stage(k, p2, st_a, st_b):
            rb, qb = statsb[k % 2]
            st_conv = P.op("ACT", lambda rb=rb, p2=p2: nc.scalar.copy(
                rb[:, :], r2s[p2][:, :]),
                deps=[st_a, st_b] + list(statsb_free[k % 2]))
            st_sq = P.op("ACT", lambda qb=qb, p2=p2: nc.scalar.activation(
                out=qb[:, :], in_=r2s[p2][:, :], func=AF.Square, scale=1.0),
                deps=[st_a, st_b] + list(statsb_free[k % 2]))
            st_lo = P.op("DVE", lambda k=k, p2=p2, rb=rb: nc.vector.tensor_sub(
                lo_scr[k % 2][:, :], r2s[p2][:, :], rb[:, :]),
                deps=[st_a, st_b, st_conv, lo_free[k % 2], ysc_free[k % 2]])
            last = None
            for h in range(2):
                po = PT[0:1, h * 512:(h + 1) * 512]
                rh = rb[:, h * HALF:(h + 1) * HALF]
                last = P.op("PE", lambda po=po, rh=rh, k=k: nc.tensor.matmul(
                    po, ones_cb[:, :], rh, start=(k == 0), stop=(k == ND - 1)),
                    deps=[st_conv] + ([st_sumr, st_sqr] if k == 0 else []), inc=0)
                po2 = PT[0:1, 1024 + h * 512: 1536 + h * 512]
                rh2 = qb[:, h * HALF:(h + 1) * HALF]
                last = P.op("PE", lambda po2=po2, rh2=rh2, k=k: nc.tensor.matmul(
                    po2, ones_cb[:, :], rh2, start=(k == 0), stop=(k == ND - 1)),
                    deps=[st_sq], inc=(1 if h == 1 else 0))
            dsth = r2hi_park[k * 128:(k + 1) * 128, :]
            st_pkh = P.dma("ACTQ", f"cRH{k % 2}", lambda dsth=dsth, rb=rb:
                           nc.scalar.dma_start(out=dsth, in_=rb[:, :]),
                           deps=[st_conv])
            dstl = r2lo_park[k * 128:(k + 1) * 128, :]
            st_pkl = P.dma("ACTQ", f"cRL{k % 2}", lambda dstl=dstl, k=k:
                           nc.scalar.dma_start(out=dstl, in_=lo_scr[k % 2][:, :]),
                           deps=[st_lo])
            statsb_free[k % 2] = (last, st_pkh)
            lo_free[k % 2] = st_pkl
            st_r2pk[k] = (st_pkh, st_pkl)
            r2s_free[p2] = (st_sq, st_lo)
            return last

        dn_ring = [wT[0], wT[1], wnat[0], wnat[1]]
        dn_free = [g_gTF[0], g_uTF[0], g_gnatF[0], g_unatF[0]]
        rdn_ring = [rdn_nat, rdnT]
        rdn_dst_free = [rdn_free[0], rdn_free[1]]
        for k in range(ND):
            p2 = k % 2
            src = h1_park[k * 128:(k + 1) * 128, :]
            st_jit = P.dma("ACTQ", f"cJ{p2}", lambda src=src, p2=p2: nc.scalar.dma_start(
                out=h1j[p2][:, :], in_=src), deps=[st_h1_parked, h1j_free[p2]])
            rr = k % 2
            rdnT_t = rdn_ring[rr]
            rsrc = rdown_bf[k * 128:(k + 1) * 128, :]
            rdst = rdnT_t.ap().rearrange("p (c i) -> p c i", c=8)
            st_rT = P.dma("SPQ", f"cRT{rr}", lambda rdst=rdst, rsrc=rsrc: nc.sync.dma_start(
                out=rdst, in_=rsrc, transpose=True),
                deps=[st_ppre_r[0], st_ppre_r[1], rdn_dst_free[rr]])
            st_yb = {}
            for i, s in enumerate(range(NRES, NSH)):
                dst = yback[i]
                src2 = ysh_park[i * 128:(i + 1) * 128, :]
                st_yb[s] = P.dma("ACTQ", f"cYB{i}", lambda dst=dst, src2=src2: nc.scalar.dma_start(
                    out=dst[:, :], in_=src2),
                    deps=st_ysh_parked_all + [yb_free[i]])
            hslot = [next_ph(), next_ph()]
            mm_last = [None, None]
            st_q = None
            for q in range(4):
                ri4 = (4 * k + q) % 4
                wTq = dn_ring[ri4]
                src3 = down_bf[k * 128:(k + 1) * 128, q * 2048:(q + 1) * 2048]
                tdst = wTq.ap().rearrange("p (c i) -> p c i", c=16)
                stT = P.dma("SPQ", f"cDT{ri4}", lambda tdst=tdst, src3=src3:
                            nc.sync.dma_start(out=tdst, in_=src3, transpose=True),
                            deps=[st_ppre[k], dn_free[ri4]])
                for h in range(2):
                    pi = hslot[h]
                    for c in range(16):
                        s = q * 16 + c
                        lhs = wTq[:, c * 128:(c + 1) * 128]
                        if s < NRES:
                            rhs = y_sh[:, s * 1024 + h * HALF: s * 1024 + (h + 1) * HALF]
                            rdep = st_ysh_all if (c == 0 and h == 0) else None
                        else:
                            rhs = yback[s - NRES][:, h * HALF:(h + 1) * HALF]
                            rdep = st_yb[s]
                        is_last = (h == 1 and c == 15)
                        mm_last[h] = P.op("PE", lambda lhs=lhs, rhs=rhs, pi=pi, q=q, c=c:
                                          nc.tensor.matmul(
                            PH[pi][:, :], lhs, rhs,
                            start=(q == 0 and c == 0), stop=False),
                            deps=([stT, rdep, ph_free[pi]] if c == 0 else [rdep]),
                            inc=(1 if is_last else 0))
                        if not is_last and mm_last[h] is None:
                            mm_last[h] = None
                st_q = mm_last[1]
                dn_free[ri4] = st_q
            for i in range(NSH - NRES):
                yb_free[i] = st_q
            for h in range(2):
                pi = hslot[h]
                for c in range(NR):
                    lhs = rdnT_t[:, c * 128:(c + 1) * 128]
                    rhs = y_r2[:, c * 1024 + h * HALF: c * 1024 + (h + 1) * HALF]
                    mm_last[h] = P.op("PE", lambda lhs=lhs, rhs=rhs, pi=pi, c=c:
                                      nc.tensor.matmul(
                        PH[pi][:, :], lhs, rhs, start=False, stop=(c == NR - 1)),
                        deps=([st_rT, st_yrb_all] if c == 0 else []),
                        inc=(1 if c == NR - 1 else 0))
            rdn_dst_free[rr] = mm_last[1]
            st_r2h = [None, None]
            for h in range(2):
                pi = hslot[h]
                dst = r2s[p2][:, h * HALF:(h + 1) * HALF]
                st_r2h[h] = P.op("DVE", lambda dst=dst, pi=pi, p2=p2, h=h:
                                 nc.vector.tensor_add(
                    dst, PH[pi][:, :], h1j[p2][:, h * HALF:(h + 1) * HALF]),
                    deps=[mm_last[h], st_jit] +
                          (list(r2s_free[p2]) if h == 0 else []))
                ph_free[pi] = st_r2h[h]
            h1j_free[p2] = st_r2h[1]
            if pending_stats2 is not None:
                st_stats2_pe = emit_stats2_stage(*pending_stats2)
            pending_stats2 = (k, p2, st_r2h[0], st_r2h[1])

        if pending_stats2 is not None:
            st_stats2_pe = emit_stats2_stage(*pending_stats2)
            pending_stats2 = None
        P.mark("F")
        st_sumr2 = P.op("DVE", lambda: nc.vector.tensor_copy(
            sum_row[:, :], PT[0:1, 0:1024]), deps=[st_stats2_pe, st_meanB])
        st_sqr2 = P.op("DVE", lambda: nc.vector.tensor_copy(
            sq_row[:, :], PT[0:1, 1024:2048]), deps=[st_stats2_pe, st_rstdB])
        st_mean2 = P.op("DVE", lambda: nc.vector.tensor_scalar_mul(
            sum_row[:, :], sum_row[:, :], 1.0 / D), deps=[st_sumr2])
        st_msq2 = P.op("DVE", lambda: nc.vector.tensor_scalar_mul(
            sq_row[:, :], sq_row[:, :], 1.0 / D), deps=[st_sqr2])
        st_m22 = P.op("DVE", lambda: nc.vector.tensor_tensor(
            scr[4][0:1, :], sum_row[:, :], sum_row[:, :], op=ALU.mult),
            deps=[st_mean2, ysc_free[0], st_r2pk[-1][1], st_r2pk[-2][1]])
        st_var2 = P.op("DVE", lambda: nc.vector.tensor_sub(
            sq_row[:, :], sq_row[:, :], scr[4][0:1, :]), deps=[st_m22, st_msq2])
        st_std2 = P.op("ACT", lambda: nc.scalar.activation(
            out=sq_row[:, :], in_=sq_row[:, :], func=AF.Sqrt, bias=1e-5, scale=1.0),
            deps=[st_var2, st_eps])
        st_rstd2 = P.op("DVE", lambda: nc.vector.reciprocal(
            sq_row[:, :], sq_row[:, :]), deps=[st_std2])
        st_mc = []
        for j in range(NT):
            po = PH[0][:, j:j + 1]
            si = sum_row[0:1, j * 128:(j + 1) * 128]
            stm = P.op("PE", lambda po=po, si=si: nc.tensor.transpose(
                po, si, ident_f[0:1, 0:1]), deps=[st_mean2, ph_free[0]])
            po2 = PH[1][:, j:j + 1]
            si2 = sq_row[0:1, j * 128:(j + 1) * 128]
            stm2 = P.op("PE", lambda po2=po2, si2=si2: nc.tensor.transpose(
                po2, si2, ident_f[0:1, 0:1]), deps=[st_rstd2, ph_free[1]])
            st_mc.append((stm, stm2))
        st_pk_all = list(st_r2pk[-2]) + list(st_r2pk[-1])
        st_e1 = P.op("DVE", lambda: nc.vector.tensor_copy(
            mean_cols[:, :], PH[0][:, 0:8]), deps=[st_mc[-1][0]] + st_pk_all)
        st_e2 = P.op("DVE", lambda: nc.vector.tensor_copy(
            rstd_cols[:, :], PH[1][:, 0:8]), deps=[st_mc[-1][1]])
        ph_free[0] = st_e1
        ph_free[1] = st_e2
        st_b2c = P.op("DVE", lambda: nc.vector.tensor_mul(
            bias2_cols[:, :], mean_cols[:, :], rstd_cols[:, :]), deps=[st_e1, st_e2])
        st_b2c = P.op("DVE", lambda: nc.vector.tensor_scalar_mul(
            bias2_cols[:, :], bias2_cols[:, :], -1.0), deps=[st_b2c])

        st_out = []
        hiT_free = [st_ysh_all, st_ysh_all]
        loT_free = [st_ysh_all, st_ysh_all]
        rs_free = [st_ysh_all, st_ysh_all]
        os_free = [st_ysh_all, st_ysh_all]
        for j in range(NT):
            jr = j % 2
            srch = r2hi_park.rearrange("c t -> c t")[:, j * 128:(j + 1) * 128]
            stth = P.dma("SPQ", f"cGH{jr}", lambda jr=jr, srch=srch: nc.sync.dma_start(
                out=hiT[jr][:, :], in_=srch, transpose=True),
                deps=st_pk_all + [hiT_free[jr]])
            srcl = r2lo_park[:, j * 128:(j + 1) * 128]
            sttl = P.dma("SPQ", f"cGL{jr}", lambda jr=jr, srcl=srcl: nc.sync.dma_start(
                out=loT[jr][:, :], in_=srcl, transpose=True),
                deps=st_pk_all + [loT_free[jr]])
            st_rs = P.op("DVE", lambda jr=jr: nc.vector.tensor_add(
                rsum[jr][:, :], hiT[jr][:, :], loT[jr][:, :]),
                deps=[stth, sttl, rs_free[jr]])
            hiT_free[jr] = st_rs
            loT_free[jr] = st_rs
            st_nm = P.op("ACT", lambda jr=jr, j=j: nc.scalar.activation(
                out=out_scr[jr][:, :], in_=rsum[jr][:, :], func=AF.Identity,
                bias=bias2_cols[:, j:j + 1], scale=rstd_cols[:, j:j + 1]),
                deps=[st_rs, st_b2c, os_free[jr]])
            rs_free[jr] = st_nm
            dsto = out_own[j * 128:(j + 1) * 128, :]
            sto = P.dma("ACTQ", f"cOUT{jr}", lambda dsto=dsto, jr=jr: nc.scalar.dma_start(
                out=dsto, in_=out_scr[jr][:, :]), deps=[st_nm])
            os_free[jr] = sto
            st_out.append(sto)

        P.op("SP", lambda: nc.sync.nop(), deps=st_out, inc=0)

        if stage is not None:
            cut = stage if isinstance(stage, int) else P.marks[stage]
            P.ops = P.ops[:cut]
            tcounts = {}
            for (_h, _fn, _d, _inc, _sem) in P.ops:
                tcounts[_sem] = tcounts.get(_sem, 0) + _inc
            deps_all = [(kk, vv) for kk, vv in tcounts.items() if vv]
            for j in range(NT):
                dsto = out_own[j * 128:(j + 1) * 128, 0:1024]
                P.dma("ACTQ", "cDBG", lambda dsto=dsto: nc.scalar.dma_start(
                    out=dsto, in_=scr[0][:, :]),
                    deps=(deps_all if j == 0 else []))
            P.op("SP", lambda: nc.sync.nop(), deps=[("cDBG", 16 * NT)], inc=0)

        with contextlib.ExitStack() as sems_stk:
            sem_names = sorted(P.counts.keys())
            sems = {k: sems_stk.enter_context(nc.semaphore("sem_" + k))
                    for k in sem_names}
            block = sems_stk.enter_context(nc.Block())

            def emit_for(host):
                known = {}
                eng = {"PE": nc.tensor, "DVE": nc.vector, "ACT": nc.scalar,
                       "POOL": nc.gpsimd, "SP": nc.sync}[host]
                for (h, fn, deps, inc, sem_name) in P.ops:
                    if h != host:
                        continue
                    for (dn, dv) in deps:
                        if known.get(dn, 0) >= dv:
                            continue
                        eng.wait_ge(sems[dn], dv)
                        known[dn] = dv
                    instr = fn()
                    if inc:
                        instr.then_inc(sems[sem_name], inc)

            @block.tensor
            def _(t):
                emit_for("PE")

            @block.vector
            def _(v):
                emit_for("DVE")

            @block.scalar
            def _(s):
                emit_for("ACT")

            @block.gpsimd
            def _(g):
                emit_for("POOL")

            @block.sync
            def _(s):
                emit_for("SP")

    return nc


_NC_CACHE = None


def _get_nc():
    global _NC_CACHE
    if _NC_CACHE is None:
        _NC_CACHE = build_program()
    return _NC_CACHE


def kernel(L0_attn_x, L0_attn_W, L0_attn_b, L0_ffn_router_weight,
           L0_ffn_sh_gate, L0_ffn_sh_up, L0_ffn_sh_down,
           L0_ffn_r_gate, L0_ffn_r_up, L0_ffn_r_down):
    x = np.ascontiguousarray(np.asarray(L0_attn_x, dtype=np.float32))
    w = np.ascontiguousarray(np.asarray(L0_attn_W, dtype=np.float32))
    bb = np.ascontiguousarray(np.asarray(L0_attn_b, dtype=np.float32))
    rw = np.ascontiguousarray(np.asarray(L0_ffn_router_weight, dtype=np.float32))
    sg = np.ascontiguousarray(np.asarray(L0_ffn_sh_gate, dtype=np.float32))
    su = np.ascontiguousarray(np.asarray(L0_ffn_sh_up, dtype=np.float32))
    sd = np.ascontiguousarray(np.asarray(L0_ffn_sh_down, dtype=np.float32))
    rg = np.ascontiguousarray(np.asarray(L0_ffn_r_gate, dtype=np.float32))
    ru = np.ascontiguousarray(np.asarray(L0_ffn_r_up, dtype=np.float32))
    rd = np.ascontiguousarray(np.asarray(L0_ffn_r_down, dtype=np.float32))

    zeros_warm = np.zeros((WARM, D), dtype=np.float32)
    in_maps = []
    for c in range(NCORE):
        bi, hi = c // 2, c % 2
        xo = np.ascontiguousarray(x[bi, hi * TPC:(hi + 1) * TPC, :])
        xw = zeros_warm if hi == 0 else np.ascontiguousarray(
            x[bi, hi * TPC - WARM: hi * TPC, :])
        in_maps.append({
            "x_own": xo, "x_warm": xw, "w_attn": w, "b_attn": bb,
            "router_w": rw, "sh_gate": sg, "sh_up": su, "sh_down": sd,
            "r_gate": rg, "r_up": ru, "r_down": rd,
        })

    nc = _get_nc()
    res = run_bass_kernel_spmd(nc, in_maps, core_ids=list(range(NCORE)))
    out = np.empty((B, S, D), dtype=np.float32)
    for c in range(NCORE):
        bi, hi = c // 2, c % 2
        out[bi, hi * TPC:(hi + 1) * TPC, :] = res.results[c]["out_own"]
    return out
